# version 24
# baseline (speedup 1.0000x reference)
import numpy as np
import ml_dtypes
from contextlib import ExitStack
import concourse.bass as bass
import concourse.mybir as mybir
from concourse.bass_utils import run_bass_kernel_spmd

F32 = mybir.dt.float32
BF16 = mybir.dt.bfloat16
AF = mybir.ActivationFunctionType
OP = mybir.AluOpType
AX = mybir.AxisListType
BFD = ml_dtypes.bfloat16

B, T, C = 2, 16384, 1024
H, HS = 16, 64
NCORE = 8
SLAB = T // 4
L = 64
TT1 = 256
NT1 = SLAB // TT1
CH1 = TT1 // L
NB1 = TT1 // 128
TT3A = 512
NT3A = SLAB // TT3A
NB3A = TT3A // 128
TT3B = 128
NT3B = SLAB // TT3B
NCH = SLAB // L
E, FF = 4, 4096

DBG_TILE = -1
DBG_TILE3A = -1
PHASES = "1234"
TILE_LIM = None
STAGE_LIM = 99
SAFE_CHAIN = True
RACE_DET = True
S1_NO = set()
OP_LIM = 10 ** 9
OP_SKIP = set()
SKIP_ENG = set()
TRACE_RUN = False


class Prog:
    CENG = ("pe", "dve", "act", "pool", "sp")
    DMAQ = ("sp", "act", "pool")

    def __init__(self, nc, sems, dsem):
        self.nc = nc
        self.sems = sems
        self.dsem = dsem
        self.ops = []
        self.last_write = {}
        self.readers = {}

    def op(self, engine, fn, reads=(), writes=(), dma=False, tag="", skey=None,
           step=None):
        if len(self.ops) >= OP_LIM:
            return -1
        if getattr(self, "_rawidx", None) is None:
            self._rawidx = 0
        self._rawidx += 1
        if (self._rawidx - 1) in OP_SKIP:
            return -1
        if not dma and engine in SKIP_ENG:
            return -1
        psr = tuple(r for r in reads
                    if isinstance(r, str) and r.startswith("ps") and r not in writes)
        if psr:
            writes = tuple(writes) + psr
        opid = len(self.ops)
        deps = set()
        for r in reads:
            if r in self.last_write:
                deps.add(self.last_write[r])
        for w in writes:
            if w in self.last_write:
                deps.add(self.last_write[w])
            for rd in self.readers.get(w, ()):
                deps.add(rd)
        deps.discard(opid)
        if dma and skey is None:
            skey = ("dma", engine) + tuple(writes[:1] if writes else reads[:1])
        self.ops.append(dict(id=opid, eng=engine, fn=fn, deps=deps, dma=dma,
                             tag=tag, needed=False, skey=skey,
                             step=(step if step is not None else (16 if dma else 1))))
        for r in reads:
            self.readers.setdefault(r, []).append(opid)
        for w in writes:
            self.last_write[w] = opid
            self.readers[w] = []
        return opid

    def dma(self, queue, out, in_, reads=(), writes=(), tag="", skey=None):
        return self.op(queue, lambda e, o=out, i=in_: e.dma_start(out=o, in_=i),
                       reads=reads, writes=writes, dma=True, tag=tag, skey=skey)

    def prepare(self):
        for o in self.ops:
            if o["dma"]:
                o["needed"] = True
                if o["skey"] not in self.dsem:
                    pool = self.dsem["__pool__"]
                    idx = len(self.dsem) - 1
                    assert idx < len(pool), "dma sem pool exhausted"
                    self.dsem[o["skey"]] = [pool[idx], 0]
        for o in self.ops:
            for d in o["deps"]:
                pp = self.ops[d]
                if SAFE_CHAIN or pp["dma"] or pp["eng"] != o["eng"]:
                    pp["needed"] = True
        cum = {}
        for o in self.ops:
            key = o["skey"] if o["dma"] else o["eng"]
            cum[key] = cum.get(key, 0) + (o["step"] if o["needed"] else 0)
            o["cum"] = cum[key]
        self.totals = dict(cum)
        self.streams = {e: [] for e in self.CENG}
        self.queue_keys = {e: [] for e in self.DMAQ}
        for o in self.ops:
            self.streams[o["eng"]].append(o)
            if o["dma"] and o["skey"] not in self.queue_keys[o["eng"]]:
                self.queue_keys[o["eng"]].append(o["skey"])

    def run_stream(self, eng_obj, eng_name):
        sems = self.sems
        for e2 in self.CENG:
            if e2 != eng_name:
                sem, base = sems[e2]
                if base > 0:
                    eng_obj.wait_ge(sem, base)
        clock = {}
        for o in self.streams[eng_name]:
            for d in sorted(o["deps"]):
                pp = self.ops[d]
                if not SAFE_CHAIN and pp["eng"] == eng_name and not pp["dma"]:
                    continue
                if pp["dma"]:
                    key = pp["skey"]
                    sem, base = self.dsem[key]
                else:
                    key = pp["eng"]
                    sem, base = sems[key]
                need = pp["cum"]
                if clock.get(key, 0) >= need:
                    continue
                eng_obj.wait_ge(sem, base + need)
                clock[key] = need
            inst = o["fn"](eng_obj)
            if o["needed"]:
                if o["dma"]:
                    sem, base = self.dsem[o["skey"]]
                    inst.then_inc(sem, o["step"])
                else:
                    sem, base = sems[o["eng"]]
                    inst.then_inc(sem, o["step"])
        if eng_name in self.DMAQ:
            for key in self.queue_keys[eng_name]:
                sem, base = self.dsem[key]
                eng_obj.wait_ge(sem, base + self.totals.get(key, 0))
        sem, base = sems[eng_name]
        eng_obj.wait_ge(sem, base + self.totals.get(eng_name, 0))
        eng_obj.sem_inc(sem, 1)


def run_phase(nc, sems, dsem, build_fn, name):
    p = Prog(nc, dict(sems), dsem)
    build_fn(p)
    p.prepare()
    with nc.Block(name) as blk:
        blk.sync(lambda e: p.run_stream(e, "sp"))
        blk.tensor(lambda e: p.run_stream(e, "pe"))
        blk.vector(lambda e: p.run_stream(e, "dve"))
        blk.scalar(lambda e: p.run_stream(e, "act"))
        blk.gpsimd(lambda e: p.run_stream(e, "pool"))
    new = {}
    for k, (sem, base) in sems.items():
        new[k] = (sem, base + p.totals.get(k, 0) + 1)
    for key in dsem:
        if key != "__pool__":
            dsem[key][1] += p.totals.get(key, 0)
    return new


def ln_stats(p, stat, off, xsrc, sq_s, reads, n, epsc, negc, tag=""):
    sums = stat[:, off + 0:off + 1]
    sqs = stat[:, off + 1:off + 2]
    mean = stat[:, off + 2:off + 3]
    nv = stat[:, off + 3:off + 4]
    rstd = stat[:, off + 4:off + 5]
    mb = stat[:, off + 5:off + 6]
    st = ("st", off, tag)
    p.op("dve", lambda e: e.reduce_sum(sums, xsrc, axis=AX.X), reads=reads, writes=(st,))
    if "sqacc" in S1_NO:
        p.op("dve", lambda e: e.memset(sqs, 1000.0), reads=reads, writes=(st,))
    else:
        p.op("act", lambda e: e.activation(sq_s, xsrc, AF.Square, accum_out=sqs),
             reads=reads, writes=("sq_scr", st))
    p.op("dve", lambda e: e.tensor_scalar(mean, sums, 1.0 / n, None, OP.mult),
         reads=(st,), writes=(st,))
    p.op("dve", lambda e: e.tensor_scalar(nv, sqs, 1.0 / n, None, OP.mult),
         reads=(st,), writes=(st,))
    p.op("dve", lambda e: e.scalar_tensor_tensor(nv, mean, mean, nv, OP.mult,
                                                 OP.subtract),
         reads=(st,), writes=(st,))
    if "sqrt" in S1_NO:
        p.op("dve", lambda e: e.tensor_scalar(rstd, nv, 0.0, 1.0, OP.mult, OP.add),
             reads=(st, "cv"), writes=(st,))
    else:
        p.op("act", lambda e: e.activation(rstd, nv, AF.Sqrt, bias=epsc, scale=negc),
             reads=(st, "cv"), writes=(st,))
    p.op("dve", lambda e: e.reciprocal(rstd, rstd), reads=(st,), writes=(st,))
    p.op("dve", lambda e: e.tensor_scalar(mb, mean, rstd, -1.0, OP.mult, OP.mult),
         reads=(st,), writes=(st,))
    return rstd, mb, st


def build_nc():
    nc = bass.Bass("TRN2", target_bir_lowering=False,
                   detect_race_conditions=RACE_DET)
    W = {}

    def din(name, shape, dt=F32):
        W[name] = nc.dram_tensor(name, list(shape), dt, kind="ExternalInput").ap()

    def dram(name, shape, dt):
        W[name] = nc.dram_tensor(name, list(shape), dt, kind="Internal").ap()

    din("xs", (128 + SLAB, C))
    din("wkT", (C, C)); din("wrT", (C, C)); din("wvT", (C, C)); din("wgT", (C, C))
    din("woT", (C, C))
    din("alora", (C, 160))
    din("blora33", (5, 33, C))
    din("tdA", (C, 64))
    din("tdB65", (65, C))
    din("miux_col", (128, 8))
    din("u_col", (128, 8))
    din("gam_bc", (128, C)); din("bet_bc", (128, C))
    din("maskU2", (128, L))
    din("i128f", (128, 128)); din("i128b", (128, 128), BF16)
    din("selswap", (128, 128), BF16)
    din("hind8", (8, 128, 16), BF16)
    din("prevmask", (128, 8))
    din("cvec", (128, 4))
    din("wiT", (C, FF))
    din("woFT", (FF, C))
    din("wgateT", (C, E))
    din("bg_bc", (128, E))
    din("aAllT128", (C, 128))
    for _p in range(2):
        din(f"bT8P{_p}", (128, FF))
    out_ap = nc.dram_tensor("out", [SLAB, C], F32, kind="ExternalOutput").ap()

    dram("sp_y", (SLAB, C), BF16)
    dram("sp_r", (NT1, 8, 128, TT1), BF16)
    dram("sp_xg", (NT1, 8, 128, TT1), BF16)
    dram("sp_l2", (SLAB, C), BF16)
    dram("cc_in", (128, 520), F32)
    dram("cc_out", (1024, 520), F32)

    dbg_names = []
    def dbg_out(name, shape, dt):
        dbg_names.append(name)
        return nc.dram_tensor("dbg_" + name, list(shape), dt,
                              kind="ExternalOutput").ap()

    sems = {}
    for e in ("pe", "dve", "act", "pool", "sp"):
        sems[e] = (nc.alloc_semaphore(name="s_" + e), 0)
    dsem_pool = [nc.alloc_semaphore(name=f"d{i}") for i in range(96)]
    dsem = {"__pool__": dsem_pool}

    with nc.Block("init") as blk:
        def mk(engname):
            def f(eng):
                eng.sem_clear(sems[engname][0])
                if engname == "sp":
                    for s in dsem_pool:
                        eng.sem_clear(s)
                eng.sem_inc(sems[engname][0], 1)
                for e2 in ("pe", "dve", "act", "pool", "sp"):
                    eng.wait_ge(sems[e2][0], 1)
            return f
        blk.sync(mk("sp")); blk.tensor(mk("pe")); blk.vector(mk("dve"))
        blk.scalar(mk("act")); blk.gpsimd(mk("pool"))
    for e in ("pe", "dve", "act", "pool", "sp"):
        sems[e] = (sems[e][0], 1)

    with ExitStack() as big:
        sb = lambda n, sh, dt=F32: big.enter_context(nc.sbuf_tensor(n, list(sh), dt)).ap()
        S_f = sb("S_f", (128, 8, 64))
        S_b = sb("S_b", (128, 8, 64), BF16)
        s0_f = sb("s0_f", (128, 8, 64))
        dm_fm = sb("dm_fm", (128, 8, NCH))
        dmrun = sb("dmrun", (128, 8))
        miux_c = sb("miux_c", (128, 8))
        u_c = sb("u_c", (128, 8))
        maskU_c = sb("maskU_c", (128, L))
        i128f_c = sb("i128f_c", (128, 128))
        i128b_c = sb("i128b_c", (128, 128), BF16)
        selsw_c = sb("selsw_c", (128, 128), BF16)
        hind_c = sb("hind_c", (128, 8, 16), BF16)
        pmask_c = sb("pmask_c", (128, 8))
        gam_c = sb("gam_c", (128, C))
        bet_c = sb("bet_c", (128, C))
        zz64 = sb("zz64", (128, 64))
        cv = sb("cv", (128, 4))
        NEGC = cv[:, 0:1]
        EPS1 = cv[:, 1:2]
        EPSG = cv[:, 2:3]

        with ExitStack() as st1:
            sb1 = lambda n, sh, dt=F32: st1.enter_context(nc.sbuf_tensor(n, list(sh), dt)).ap()
            ps1 = lambda n, sh, dt=F32: st1.enter_context(nc.psum_tensor(n, list(sh), dt)).ap()
            wkT_b = sb1("wkT_b", (128, 8, C), BF16)
            wrT_b = sb1("wrT_b", (128, 8, C), BF16)
            wvT_b = sb1("wvT_b", (128, 8, C), BF16)
            alora_b = sb1("alora_b", (128, 8, 160), BF16)
            blora_b = [sb1(f"blora_b{f}", (33, C), BF16) for f in range(5)]
            tdA_b = sb1("tdA_b", (128, 8, 64), BF16)
            tdB_b = sb1("tdB_b", (65, C), BF16)
            x_tm = [sb1(f"x_tm{i}", (128, NB1, C)) for i in range(2)]
            xn_tm = sb1("xn_tm", (128, NB1, C))
            sq_s = sb1("sq_s", (128, C), BF16)
            stat = sb1("stat", (128, 16))
            xn_fm = sb1("xn_fm", (128, 8, TT1 + 8))
            xnb = sb1("xnb", (128, 8, TT1), BF16)
            halo = sb1("halo", (128, 8))
            dxa_b = sb1("dxa_b", (128, 8, TT1), BF16)
            xm_b = sb1("xm_b", (128, 8, TT1), BF16)
            xx_b = [sb1(f"xx_b{i}", (128, 8, TT1), BF16) for i in range(2)]
            lh_b = [sb1(f"lh_b{f}", (33, TT1), BF16) for f in range(5)]
            hw_b = sb1("hw_b", (65, TT1), BF16)
            k_b = sb1("k_b", (128, 8, TT1), BF16)
            r_b = sb1("r_b", (128, 8, TT1), BF16)
            kt_b = sb1("kt_b", (128, 8, TT1), BF16)
            rt_b = sb1("rt_b", (128, 8, TT1), BF16)
            khf_b = sb1("khf_b", (128, 8, TT1), BF16)
            khtm_b = sb1("khtm_b", (128, NB1, C), BF16)
            v_tm = sb1("v_tm", (128, NB1, C), BF16)
            v_sw = sb1("v_sw", (128, NB1, C), BF16)
            khsw = sb1("khsw", (128, NB1, C), BF16)
            ystg = sb1("ystg", (128, 512), BF16)
            ew_f = sb1("ew_f", (128, 8, TT1))
            d_f = ew_f
            ash = sb1("ash", (128, 8, CH1, L + 1))
            rca = sb1("rca", (128, 8, CH1, L + 1), BF16)
            urk = xm_b
            y_tm = sb1("y_tm", (128, NB1, C))
            ptm_b = sb1("ptm_b", (128, 16 * L), BF16)
            dtmp = sb1("dtmp", (128, C), BF16)
            c_tm = sb1("c_tm", (128, NB1 * 16))
            psAB = ps1("psAB", (128, 1024))
            psA = psAB[:, 0:512]
            psB = psAB[:, 512:1024]
            psS = ps1("psS", (128, 512))
            psY = [ps1("psY1", (128, 1024)), ps1("psY2", (128, 1024))]
            psT = ps1("psT", (128, 512))

            def ln1(p, xsrc, xres, dst, off):
                rstd, mb, st = ln_stats(p, stat, off, xsrc, sq_s[:], xres, C, EPS1, NEGC)
                p.op("act", lambda e: e.activation(dst, xsrc, AF.Identity, bias=mb,
                                                   scale=rstd),
                     reads=xres + (st,), writes=("xn_tm",))

            def transpose8(p, src_tm, srcres, out_fn):
                def tp(e):
                    last = None
                    ncc = 4 if "tp4" in S1_NO else 8
                    for cc in range(ncc):
                        last = e.matmul(psAB[:, 128 * cc:128 * (cc + 1)],
                                        src_tm[:, 128 * cc:128 * (cc + 1)],
                                        i128f_c, start=True, stop=True)
                    return last
                p.op("pe", tp, reads=srcres + ("i128f",),
                     writes=("psA", "psB"), tag="tp")
                out_fn(p)

            def proj_fm(p, xb, xbres, wT, wres, dst, dres):
                for jp in range(4):
                    pst = (psA, psB)[jp % 2]
                    pres = ("psA", "psB")[jp % 2]
                    def kmm(e, jp=jp, pst=pst):
                        last = None
                        for jj in range(2):
                            j = 2 * jp + jj
                            for cc in range(8):
                                last = e.matmul(pst[:, 256 * jj:256 * (jj + 1)],
                                                wT[:, cc, 128 * j:128 * (j + 1)],
                                                xb[:, cc, :],
                                                start=(cc == 0), stop=(cc == 7))
                        return last
                    p.op("pe", kmm, reads=(xbres, wres), writes=(pres,))
                    p.op("act", lambda e, jp=jp, pst=pst: e.activation(
                        dst[:, 2 * jp:2 * jp + 2, :],
                        pst.rearrange("p (j n) -> p j n", j=2), AF.Copy),
                        reads=(pres,), writes=(dres,))

            def proj_tm(p, xb, xbres, wT, wres, dst, dres, func=AF.Copy):
                for bi in range(NB1):
                    pyp = psY[bi]
                    pyres = ("psY1", "psY2")[bi]
                    def vmm(e, bi=bi, pyp=pyp):
                        last = None
                        for cc in range(8):
                            for nn in range(2):
                                last = e.matmul(pyp[:, 512 * nn:512 * (nn + 1)],
                                                xb[:, cc, 128 * bi:128 * (bi + 1)],
                                                wT[:, cc, 512 * nn:512 * (nn + 1)],
                                                start=(cc == 0), stop=(cc == 7))
                        return last
                    p.op("pe", vmm, reads=(xbres, wres), writes=(pyres,))
                    p.op("act", lambda e, bi=bi, pyp=pyp: e.activation(
                        dst[:, bi, :], pyp[:], func),
                        reads=(pyres,), writes=(dres,))

            def front1(p, t, bufi):
                xres = (("x", bufi),)
                if STAGE_LIM < 2:
                    return
                for bi in range(NB1):
                    ln1(p, x_tm[bufi][:, bi, :], xres, xn_tm[:, bi, :], 8 * bi)
                if STAGE_LIM < 2.2:
                    return
                p.op("dve", lambda e: e.tensor_copy(xn_fm[:, :, 7:8],
                                                    halo[:, :].unsqueeze(2)),
                     reads=("halo",), writes=("xnfm",))
                if STAGE_LIM < 2.4:
                    return
                for bi in range(NB1):
                    for q4 in range(2):
                        def ftp(e, bi=bi, q4=q4):
                            last = None
                            for cc in range(4 * q4, 4 * q4 + 4):
                                last = e.matmul(
                                    psT[:, 128 * (cc % 4):128 * (cc % 4 + 1)],
                                    xn_tm[:, bi, 128 * cc:128 * (cc + 1)],
                                    i128f_c, start=True, stop=True)
                            return last
                        p.op("pe", ftp, reads=("xn_tm", "i128f"),
                             writes=("psT",), tag="tp")
                        p.op("dve", lambda e, bi=bi, q4=q4: e.tensor_copy(
                            xn_fm[:, 4 * q4:4 * q4 + 4,
                                  8 + 128 * bi:136 + 128 * bi],
                            psT.rearrange("p (j n) -> p j n", j=4)),
                            reads=("psT",), writes=("xnfm",))
                if STAGE_LIM < 2.6:
                    return
                p.op("dve", lambda e: e.tensor_copy(halo[:, :],
                                                    xn_fm[:, :, TT1 + 7:TT1 + 8].squeeze(2)),
                     reads=("xnfm",), writes=("halo",))
                p.op("dve", lambda e: e.tensor_copy(xnb[:], xn_fm[:, :, 8:TT1 + 8]),
                     reads=("xnfm",), writes=("xnb",))
                p.op("dve", lambda e: e.tensor_sub(dxa_b[:], xn_fm[:, :, 7:TT1 + 7],
                                                   xn_fm[:, :, 8:TT1 + 8]),
                     reads=("xnfm",), writes=("dxa",))
                if STAGE_LIM < 3:
                    return
                p.op("dve", lambda e: e.tensor_mul(
                    xm_b[:], dxa_b[:],
                    miux_c[:, :].unsqueeze(2).broadcast_to([128, 8, TT1])),
                    reads=("dxa", "miux"), writes=("xm", "urk"))
                p.op("dve", lambda e: e.tensor_add(xm_b[:], xm_b[:], xnb[:]),
                     reads=("xm", "xnb"), writes=("xm", "urk"))

            def tile1(p, t, bufi, mid=None):
                if STAGE_LIM < 4:
                    return

                def lmm(e):
                    last = None
                    for f in range(5):
                        pst = (psAB[0:32, 0:256], psAB[0:32, 256:512],
                               psAB[0:32, 512:768], psAB[0:32, 768:1024],
                               psS[0:32, 0:256])[f]
                        for cc in range(8):
                            last = e.matmul(pst, alora_b[:, cc, 32 * f:32 * f + 32],
                                            xm_b[:, cc, :], start=(cc == 0), stop=(cc == 7))
                    return last
                p.op("pe", lmm, reads=("xm", "al"), writes=("psA", "psB", "psS"))
                for f in range(5):
                    pst = (psAB[0:32, 0:256], psAB[0:32, 256:512],
                           psAB[0:32, 512:768], psAB[0:32, 768:1024],
                           psS[0:32, 0:256])[f]
                    pres = ("psA", "psA", "psB", "psB", "psS")[f]
                    p.op("act", lambda e, f=f, pst=pst: e.activation(
                        lh_b[f][0:32, :], pst, AF.Tanh),
                        reads=(pres,), writes=(("lh", f),))

                if STAGE_LIM < 5:
                    return
                pending = [None]
                for f in range(5):
                    if STAGE_LIM < 5 + f:
                        break
                    xb = xx_b[f % 2]
                    xbres = ("xx", f % 2)
                    for half in range(2):
                        pyp = psY[half]
                        pyres = ("psY1", "psY2")[half]
                        def mmm(e, f=f, half=half, pyp=pyp):
                            last = None
                            for j in range(4):
                                cj = 4 * half + j
                                last = e.matmul(pyp[:, 256 * j:256 * (j + 1)],
                                                blora_b[f][0:33, 128 * cj:128 * (cj + 1)],
                                                lh_b[f][0:33, :], start=True, stop=True)
                            return last
                        p.op("pe", mmm, reads=(("lh", f), f"bl{f}"), writes=(pyres,))

                        p.op("dve", lambda e, half=half, pyp=pyp, xb=xb: e.tensor_mul(
                            xb[:, 4 * half:4 * half + 4, :],
                            dxa_b[:, 4 * half:4 * half + 4, :],
                            pyp.rearrange("p (j n) -> p j n", j=4)),
                            reads=("dxa", ("psY1", "psY2")[half]), writes=(xbres,))
                    p.op("dve", lambda e, xb=xb: e.tensor_add(xb[:], xb[:], xnb[:]),
                         reads=(xbres, "xnb"), writes=(xbres,))
                    if pending[0] is not None:
                        pending[0]()
                        pending[0] = None

                    def wmm2_rnd(rnd):
                        for half in range(2):
                            pst = (psA, psB)[half]
                            pres = ("psA", "psB")[half]
                            def wmm2(e, rnd=rnd, half=half, pst=pst):
                                last = None
                                for j in range(2):
                                    cj = 4 * rnd + 2 * half + j
                                    last = e.matmul(pst[:, 256 * j:256 * (j + 1)],
                                                    tdB_b[0:65, 128 * cj:128 * (cj + 1)],
                                                    hw_b[0:65, :], start=True, stop=True)
                                return last
                            p.op("pe", wmm2, reads=("hw", "tdB"), writes=(pres,))
                            p.op("act", lambda e, rnd=rnd, half=half, pst=pst:
                                 e.activation(
                                     ew_f[:, 4 * rnd + 2 * half:4 * rnd + 2 * half + 2, :],
                                     pst.rearrange("p (j n) -> p j n", j=2), AF.Exp),
                                 reads=(pres,), writes=("ew",))

                    def extras(f=f, xb=xb, xbres=xbres):
                      if f == 0:
                          def wmm1(e, xb=xb):
                              last = None
                              for cc in range(8):
                                  last = e.matmul(psS[0:64, 0:TT1], tdA_b[:, cc, :],
                                                  xb[:, cc, :], start=(cc == 0), stop=(cc == 7))
                              return last
                          p.op("pe", wmm1, reads=(xbres, "tdA"), writes=("psS",))
                          p.op("act", lambda e: e.activation(hw_b[0:64, :],
                                                             psS[0:64, 0:TT1], AF.Tanh),
                               reads=("psS",), writes=("hw",))
                      elif f == 1:
                          wmm2_rnd(0)
                          proj_fm(p, xb, xbres, wkT_b, "wk", k_b, "k")
                      elif f == 2:
                          wmm2_rnd(1)
                          proj_tm(p, xb, xbres, wvT_b, "wv", v_tm, "v")
                      elif f == 3:
                          p.op("act", lambda e: e.activation(d_f[:], ew_f[:], AF.Exp,
                                                             scale=-1.0),
                               reads=("ew",), writes=("ew", "d"))
                          def scn(e):
                              last = None
                              for cc in range(8):
                                  for m in range(CH1):
                                      last = e.tensor_tensor_scan(
                                          ash[:, cc, m, 1:L + 1],
                                          d_f[:, cc, L * m:L * (m + 1)],
                                          zz64[:, 0:L], 1.0, OP.mult, OP.add)
                              return last
                          p.op("dve", scn, reads=("d", "zz"), writes=("ash",))
                          def rcp(e):
                              with nc.allow_low_precision(reason="1/A as bf16 matmul operand"):
                                  return e.reciprocal(
                                      rca.rearrange("p c m l -> p (c m l)"),
                                      ash.rearrange("p c m l -> p (c m l)"))
                          p.op("dve", rcp, reads=("ash",), writes=("rca",))
                          proj_fm(p, xb, xbres, wrT_b, "wr", r_b, "r")
                      elif f == 4:
                          p.dma("pool", W["sp_xg"][t].rearrange("c p n -> p c n"), xb[:],
                                reads=(xbres,), writes=(), tag="spxg")
                    pending[0] = extras


                if pending[0] is not None:
                    pending[0]()
                    pending[0] = None

                if STAGE_LIM < 11:
                    return
                kv = lambda z: z.rearrange("p c (m l) -> p c m l", l=L)
                p.op("dve", lambda e: e.tensor_mul(kv(kt_b), kv(k_b),
                                                   rca[:, :, :, 1:L + 1]),
                     reads=("k", "rca"), writes=("kt",))
                p.op("dve", lambda e: e.tensor_mul(
                    kv(khf_b), kv(kt_b),
                    ash[:, :, :, L:L + 1].broadcast_to([128, 8, CH1, L])),
                    reads=("kt", "ash"), writes=("khf",))
                p.op("dve", lambda e: e.tensor_mul(kv(rt_b), kv(r_b), ash[:, :, :, 0:L]),
                     reads=("r", "ash"), writes=("rt",))
                p.op("dve", lambda e: e.tensor_mul(urk[:], r_b[:], k_b[:]),
                     reads=("r", "k"), writes=("urk", "xm"))
                p.op("dve", lambda e: e.tensor_mul(
                    urk[:], urk[:],
                    u_c[:, :].unsqueeze(2).broadcast_to([128, 8, TT1])),
                    reads=("urk", "u"), writes=("urk",))

                def cmm(e):
                    last = None
                    for bi in range(NB1):
                        for cc in range(8):
                            last = e.matmul(psS[:, 16 * bi:16 * bi + 16],
                                            urk[:, cc, 128 * bi:128 * (bi + 1)],
                                            hind_c[:, cc, :], start=(cc == 0), stop=(cc == 7))
                    return last
                p.op("pe", cmm, reads=("urk", "hind"), writes=("psS",))
                p.op("dve", lambda e: e.tensor_copy(c_tm[:], psS[:, 0:16 * NB1]),
                     reads=("psS",), writes=("c",))

                for bi in range(NB1):
                    for q4 in range(2):
                        def ktp(e, bi=bi, q4=q4):
                            last = None
                            for cc in range(4 * q4, 4 * q4 + 4):
                                last = e.matmul(psT[:, 128 * (cc % 4):128 * (cc % 4 + 1)],
                                                khf_b[:, cc, 128 * bi:128 * (bi + 1)],
                                                i128b_c, start=True, stop=True)
                            return last
                        p.op("pe", ktp, reads=("khf", "i128b"), writes=("psT",))
                        p.op("dve", lambda e, bi=bi, q4=q4: e.tensor_copy(
                            khtm_b[:, bi, 512 * q4:512 * (q4 + 1)], psT[:]),
                            reads=("psT",), writes=("khtm",))
                    for nn in range(2):
                        p.op("pe", lambda e, bi=bi, nn=nn: e.matmul(
                            psT[:], selsw_c, khtm_b[:, bi, 512 * nn:512 * (nn + 1)],
                            start=True, stop=True),
                            reads=("khtm", "selsw"), writes=("psT",), tag="khsw")
                        p.op("dve", lambda e, bi=bi, nn=nn: e.tensor_copy(
                            khsw[:, bi, 512 * nn:512 * (nn + 1)], psT[:]),
                            reads=("psT",), writes=(("khsw", bi),), tag="khswv")
                        p.op("pe", lambda e, bi=bi, nn=nn: e.matmul(
                            psT[:], selsw_c, v_tm[:, bi, 512 * nn:512 * (nn + 1)],
                            start=True, stop=True),
                            reads=("v", "selsw"), writes=("psT",), tag="vsw")
                        p.op("dve", lambda e, bi=bi, nn=nn: e.tensor_copy(
                            v_sw[:, bi, 512 * nn:512 * (nn + 1)], psT[:]),
                            reads=("psT",), writes=(("vsw", bi),), tag="vswv")

                if STAGE_LIM < 12:
                    return
                for m in range(CH1):
                    bm = m // 2
                    mp = m % 2
                    pyp = psY[bm]
                    pyo = 512 * mp
                    pyres = ("psY1", "psY2")[bm]

                    def ptmm(e, m=m):
                        last = None
                        for h in range(H):
                            ch, hh = h // 2, (h % 2) * 64
                            last = e.matmul(psAB[hh:hh + 64, 64 * ch:64 * ch + 64],
                                            kt_b[hh:hh + 64, ch, L * m:L * (m + 1)],
                                            rt_b[hh:hh + 64, ch, L * m:L * (m + 1)],
                                            start=True, stop=True)
                        return last
                    p.op("pe", ptmm, reads=("kt", "rt"), writes=("psA", "psB"))
                    pb = ptm_b
                    p.op("dve", lambda e: e.tensor_mul(
                        ptm_b[:, 0:512].rearrange("p (c l) -> p c l", c=8),
                        psAB[:, 0:512].rearrange("p (c l) -> p c l", c=8),
                        maskU_c[:, :].unsqueeze(1).broadcast_to([128, 8, L])),
                        reads=("psA", "psB", "maskU"), writes=(("ptm", m % 2),))

                    def ymm(e, m=m, bm=bm, mp=mp, pyp=pyp, pyo=pyo):
                        last = None
                        for h in range(H):
                            ch, hh = h // 2, (h % 2) * 64
                            if hh == 64 * mp:
                                vsl = v_tm[hh:hh + 64, bm, 64 * h:64 * h + 64]
                            else:
                                vsl = v_sw[hh:hh + 64, bm, 64 * h:64 * h + 64]
                            last = e.matmul(pyp[hh:hh + 64, pyo + 64 * ch:pyo + 64 * ch + 64],
                                            ptm_b[hh:hh + 64, 64 * ch:64 * ch + 64],
                                            vsl, start=True, stop=False)
                            last = e.matmul(pyp[hh:hh + 64, pyo + 64 * ch:pyo + 64 * ch + 64],
                                            rt_b[hh:hh + 64, ch, L * m:L * (m + 1)],
                                            S_b[hh:hh + 64, ch, :], start=False, stop=True)
                        return last
                    p.op("pe", ymm, reads=(("ptm", m % 2), "v", ("vsw", bm), "rt", "Sb"),
                         writes=(pyres,))

                    def smm(e, m=m, bm=bm, mp=mp):
                        last = None
                        for h in range(H):
                            ch, hh = h // 2, (h % 2) * 64
                            if hh == 64 * mp:
                                ksl = khtm_b[hh:hh + 64, bm, 64 * h:64 * h + 64]
                                vsl = v_tm[hh:hh + 64, bm, 64 * h:64 * h + 64]
                            else:
                                ksl = khsw[hh:hh + 64, bm, 64 * h:64 * h + 64]
                                vsl = v_sw[hh:hh + 64, bm, 64 * h:64 * h + 64]
                            last = e.matmul(psS[hh:hh + 64, 64 * ch:64 * ch + 64],
                                            ksl, vsl, start=True, stop=True)
                        return last
                    p.op("pe", smm, reads=("khtm", ("khsw", bm), "v", ("vsw", bm)),
                         writes=("psS",))
                    p.op("dve", lambda e, m=m: e.tensor_mul(
                        S_f[:], S_f[:],
                        ash[:, :, m, L:L + 1].broadcast_to([128, 8, 64])),
                        reads=("S", "ash"), writes=("S",))
                    p.op("dve", lambda e: e.tensor_add(
                        S_f[:], S_f[:],
                        psS[:, 0:512].rearrange("p (c n) -> p c n", c=8)),
                        reads=("S", "psS"), writes=("S",))
                    p.op("dve", lambda e: e.tensor_copy(S_b[:], S_f[:]),
                         reads=("S",), writes=("Sb",))
                    gm = CH1 * t + m
                    p.op("dve", lambda e, gm=gm: e.tensor_copy(
                        dm_fm[:, :, gm:gm + 1], dmrun[:, :].unsqueeze(2)),
                        reads=("dmrun",), writes=("dm",))
                    p.op("dve", lambda e, m=m: e.tensor_mul(dmrun[:], dmrun[:],
                                                            ash[:, :, m, L]),
                         reads=("dmrun", "ash"), writes=("dmrun",))

                    mo = 64 * mp
                    def dg(e, bm=bm, mo=mo, m=m):
                        return e.tensor_mul(
                            dtmp[mo:mo + 64, :].rearrange("p (h n) -> p h n", h=H),
                            v_tm[mo:mo + 64, bm, :].rearrange("p (h n) -> p h n", h=H),
                            c_tm[mo:mo + 64, 16 * bm:16 * bm + 16].unsqueeze(2)
                                .broadcast_to([64, H, HS]))
                    p.op("dve", dg, reads=("v", "c"), writes=("dtmp",))
                    def yev1(e, pyp=pyp, pyo=pyo, mo=mo, bm=bm, mp=mp):
                        yv = y_tm[mo:mo + 64, bm, :].rearrange(
                            "p (c two l) -> p c two l", two=2, l=HS)[:, :, mp, :]
                        dv = dtmp[mo:mo + 64, :].rearrange(
                            "p (c two l) -> p c two l", two=2, l=HS)[:, :, mp, :]
                        return e.tensor_add(
                            yv,
                            pyp[mo:mo + 64, pyo:pyo + 512]
                            .rearrange("p (c l) -> p c l", l=HS),
                            dv)
                    p.op("dve", yev1, reads=(pyres, "dtmp"), writes=("y",))
                    def yst(e, pyp=pyp, pyo=pyo, mp=mp):
                        so = 64 * (1 - mp)
                        return e.tensor_copy(ystg[so:so + 64, :],
                                             pyp[so:so + 64, pyo:pyo + 512])
                    p.op("dve", yst, reads=(pyres,), writes=("ystg",))
                    def ysw(e):
                        with nc.allow_low_precision(reason="bf16 y swap"):
                            return e.matmul(psS[:, 0:512], selsw_c, ystg[:],
                                            start=True, stop=True)
                    p.op("pe", ysw, reads=("ystg", "selsw"), writes=("psS",), tag="ysw")
                    def yev2(e, mo=mo, bm=bm, mp=mp):
                        yv = y_tm[mo:mo + 64, bm, :].rearrange(
                            "p (c two l) -> p c two l", two=2, l=HS)[:, :, 1 - mp, :]
                        dv = dtmp[mo:mo + 64, :].rearrange(
                            "p (c two l) -> p c two l", two=2, l=HS)[:, :, 1 - mp, :]
                        return e.tensor_add(
                            yv,
                            psS[mo:mo + 64, 0:512]
                            .rearrange("p (c l) -> p c l", l=HS),
                            dv)
                    p.op("dve", yev2, reads=("psS", "dtmp"), writes=("y",))
                    if m == 1 and mid is not None:
                        mid()

                p.dma("pool", W["sp_y"][t * TT1:(t + 1) * TT1, :]
                      .rearrange("(b q) f -> q b f", q=128), y_tm[:],
                      reads=("y",), writes=(), tag="spy")
                p.dma("act", W["sp_r"][t].rearrange("c p n -> p c n"), rt_b[:],
                      reads=("rt",), writes=(), tag="spr")

                if t == DBG_TILE:
                    dbg1 = {"xn_fm": (xn_fm, "xnfm"), "xnb": (xnb, "xnb"),
                            "dxa": (dxa_b, "dxa"), "xm": (xm_b, "urk"),
                            "k": (k_b, "k"), "r": (r_b, "r"), "v": (v_tm, "v"),
                            "ew": (ew_f, "ew"), "d": (d_f, "d"), "ash": (ash, "ash"),
                            "kt": (kt_b, "kt"), "rt": (rt_b, "rt"),
                            "khf": (khf_b, "khf"), "khtm": (khtm_b, "khtm"),
                            "urk": (urk, "urk"), "c": (c_tm, "c"), "y": (y_tm, "y"),
                            "S": (S_f, "S"), "xx0": (xx_b[0], ("xx", 0)),
                            "xx1": (xx_b[1], ("xx", 1)), "dm": (dm_fm, "dm")}
                    for nm, (src, res) in dbg1.items():
                        da = dbg_out(nm, list(src.shape), src.dtype)
                        p.dma("sp", da, src, reads=(res,), tag="dbg")

            def phase1(p):
                p.dma("sp", miux_c, W["miux_col"], writes=("miux",))
                p.dma("sp", u_c, W["u_col"], writes=("u",))
                p.dma("sp", maskU_c, W["maskU2"], writes=("maskU",))
                p.dma("sp", i128f_c, W["i128f"], writes=("i128f",))
                p.dma("sp", i128b_c, W["i128b"], writes=("i128b",))
                p.dma("sp", selsw_c, W["selswap"], writes=("selsw",))
                p.dma("sp", hind_c, W["hind8"].rearrange("c p h -> p c h"),
                      writes=("hind",))
                p.dma("sp", pmask_c, W["prevmask"], writes=("pmask",))
                p.dma("sp", cv, W["cvec"], writes=("cv",))
                p.dma("pool", wkT_b, W["wkT"].rearrange("(c p) n -> p c n", p=128),
                      writes=("wk",))
                p.dma("pool", wrT_b, W["wrT"].rearrange("(c p) n -> p c n", p=128),
                      writes=("wr",))
                p.dma("pool", wvT_b, W["wvT"].rearrange("(c p) n -> p c n", p=128),
                      writes=("wv",))
                p.dma("pool", alora_b, W["alora"].rearrange("(c p) n -> p c n", p=128),
                      writes=("al",))
                for f in range(5):
                    p.dma("pool", blora_b[f], W["blora33"][f], writes=(f"bl{f}",))
                p.dma("pool", tdA_b, W["tdA"].rearrange("(c p) n -> p c n", p=128),
                      writes=("tdA",))
                p.dma("pool", tdB_b, W["tdB65"], writes=("tdB",))

                p.op("dve", lambda e: e.memset(S_f[:], 0.0), writes=("S",))
                p.op("dve", lambda e: e.memset(xn_fm[:, :, 0:8], 0.0), writes=("xnfm",))
                p.op("dve", lambda e: e.memset(dm_fm[:], 0.0), writes=("dm",))
                p.op("dve", lambda e: e.memset(ew_f[:], 0.0), writes=("ew",))
                p.op("dve", lambda e: e.memset(ash[:], 0.0), writes=("ash",))
                p.op("dve", lambda e: e.memset(ystg[:], 0.0), writes=("ystg",))
                p.op("dve", lambda e: e.memset(S_b[:], 0.0), writes=("Sb",))
                p.op("dve", lambda e: e.memset(dmrun[:], 1.0), writes=("dmrun",))
                p.op("dve", lambda e: e.memset(halo[:], 0.0), writes=("halo",))
                p.op("dve", lambda e: e.memset(ash[:, :, :, 0:1], 1.0), writes=("ash",))
                p.op("dve", lambda e: e.memset(zz64[:], 0.0), writes=("zz",))
                for f in range(5):
                    p.op("dve", lambda e, f=f: e.memset(lh_b[f][32:33, :], 1.0),
                         writes=(("lh", f),))
                p.op("dve", lambda e: e.memset(hw_b[64:65, :], 1.0), writes=("hw",))

                if STAGE_LIM < 1:
                    return
                p.dma("sp", x_tm[1][:, 0, :], W["xs"][0:128, :], writes=(("x", 1),))
                ln1(p, x_tm[1][:, 0, :], (("x", 1),), xn_tm[:, 0, :], 0)
                def hevac(p_):
                    if "hv2" in S1_NO:
                        p_.op("dve", lambda e: e.tensor_copy(
                            halo[:, 0:4],
                            psAB.rearrange("p (j n) -> p j n", j=8)[:, 0:4, 127]),
                            reads=("psA", "psB"), writes=("halo",))
                        p_.op("dve", lambda e: e.tensor_copy(
                            halo[:, 4:8],
                            psAB.rearrange("p (j n) -> p j n", j=8)[:, 4:8, 127]),
                            reads=("psA", "psB"), writes=("halo",))
                    else:
                        p_.op("dve", lambda e: e.tensor_copy(
                            halo[:, :],
                            psAB.rearrange("p (j n) -> p j n", j=8)[:, :, 127]),
                            reads=("psA", "psB"), writes=("halo",))
                transpose8(p, xn_tm[:, 0, :], ("xn_tm",), hevac)

                NT1e = TILE_LIM if TILE_LIM is not None else NT1

                def ldx(t):
                    bufi = t % 2
                    rows0 = 128 + t * TT1
                    p.dma("sp", x_tm[bufi],
                          W["xs"][rows0:rows0 + TT1, :]
                          .rearrange("(b q) f -> q b f", q=128),
                          writes=(("x", bufi),), tag=f"ldx{t}")

                ldx(0)
                front1(p, 0, 0)
                for t in range(NT1e):
                    if t + 1 < NT1e:
                        ldx(t + 1)

                    def mid(tt=t):
                        if tt + 1 < NT1e:
                            front1(p, tt + 1, (tt + 1) % 2)
                    tile1(p, t, t % 2, mid)

                p.dma("pool", W["cc_in"][:, 0:512],
                      S_f.rearrange("p c n -> p (c n)"), reads=("S",), writes=("ccin",))
                p.dma("pool", W["cc_in"][:, 512:520], dmrun[:, :],
                      reads=("dmrun",), writes=("ccin",))

            if "1" in PHASES:
                sems = run_phase(nc, sems, dsem, phase1, "p1")

        with ExitStack() as st2:
            sb2 = lambda n, sh, dt=F32: st2.enter_context(nc.sbuf_tensor(n, list(sh), dt)).ap()
            gq_b = sb2("gq_b", (128, 520))
            coef = sb2("coef", (128, 8))
            fcol = sb2("fcol", (128, 8))
            t8 = sb2("t8", (128, 8))
            wgT_b3 = sb2("wgT_b3", (128, 8, C), BF16)
            woT_b3 = sb2("woT_b3", (128, 8, C), BF16)

            def phase2(p):
                p.op("pool", lambda e: e.collective_compute(
                    "AllGather", OP.bypass,
                    replica_groups=[[0, 1, 2, 3, 4, 5, 6, 7]],
                    ins=[W["cc_in"].opt()], outs=[W["cc_out"].opt()]),
                    reads=(), writes=("ccout",), dma=True, tag="ag", step=1)
                p.dma("pool", wgT_b3, W["wgT"].rearrange("(c p) n -> p c n", p=128),
                      writes=("wg",))
                p.dma("pool", woT_b3, W["woT"].rearrange("(c p) n -> p c n", p=128),
                      writes=("wo",))
                p.dma("sp", gam_c, W["gam_bc"], writes=("gam",))
                p.dma("sp", bet_c, W["bet_bc"], writes=("bet",))
                p.op("dve", lambda e: e.memset(s0_f[:], 0.0), writes=("s0",))
                p.op("dve", lambda e: e.memset(fcol[:], 1.0), writes=("f",))
                for q in range(7, -1, -1):
                    p.dma("sp", gq_b, W["cc_out"][128 * q:128 * (q + 1), :],
                          reads=("ccout",), writes=("gq",), tag=f"gq{q}")
                    p.op("dve", lambda e, q=q: e.tensor_scalar(
                        coef[:], fcol[:], pmask_c[:, q:q + 1], None, OP.mult),
                        reads=("f", "gq"), writes=("coef",))
                    for pp in range(8):
                        p.op("dve", lambda e, pp=pp: e.scalar_tensor_tensor(
                            s0_f[:, pp, :], gq_b[:, 64 * pp:64 * pp + 64],
                            coef[:, pp:pp + 1], s0_f[:, pp, :], OP.mult, OP.add),
                            reads=("gq", "coef", "s0"), writes=("s0",))
                    p.op("dve", lambda e, q=q: e.tensor_scalar(
                        t8[:], gq_b[:, 512:520], -1.0, pmask_c[:, q:q + 1],
                        OP.add, OP.mult), reads=("gq",), writes=("t8",))
                    p.op("dve", lambda e: e.tensor_scalar(t8[:], t8[:], 1.0, None,
                                                          OP.add),
                         reads=("t8",), writes=("t8",))
                    p.op("dve", lambda e: e.tensor_mul(fcol[:], fcol[:], t8[:]),
                         reads=("f", "t8", "gq"), writes=("f",))

            if "2" in PHASES:
                sems = run_phase(nc, sems, dsem, phase2, "p2")

            with ExitStack() as st3:
                sb3 = lambda n, sh, dt=F32: st3.enter_context(nc.sbuf_tensor(n, list(sh), dt)).ap()
                ps3 = lambda n, sh, dt=F32: st3.enter_context(nc.psum_tensor(n, list(sh), dt)).ap()
                yl_b2 = [sb3(f"yl_b{i}", (128, NB3A, C), BF16) for i in range(2)]
                rt3_b2 = [sb3(f"rt3_b{i}", (128, 8, TT3A), BF16) for i in range(2)]
                xg3_b2 = [sb3(f"xg3_b{i}", (128, 8, TT3A), BF16) for i in range(2)]
                g3_tm = sb3("g3_tm", (128, NB3A, C), BF16)
                s0m_b = sb3("s0m_b", (128, 8, 64), BF16)
                yfull = sb3("yfull", (128, NB3A, C))
                sq3_s = sb3("sq3_s", (128, C), BF16)
                st3a = sb3("st3a", (128, 192))
                zt3 = sb3("zt3", (128, C))
                gp3 = sb3("gp3", (128, C), BF16)
                bp3 = sb3("bp3", (128, C), BF16)
                z3_b = sb3("z3_b", (128, NB3A, C), BF16)
                zf3_b = sb3("zf3_b", (128, 8, TT3A), BF16)
                ao3 = sb3("ao3", (128, C))
                l2_b = sb3("l2_b", (128, NB3A, C), BF16)
                psY3 = [ps3("psY31", (128, 1024)), ps3("psY32", (128, 1024))]
                psT3 = ps3("psT3", (128, 1024))
                psS3 = ps3("psS3", (128, 512))
                ystg3 = sb3("ystg3", (128, 512), BF16)

                def phase3a(p):
                    for t in range(NT3A):
                        r0 = t * TT3A
                        tp_ = t % 2
                        yl_b = yl_b2[tp_]
                        rt3_b = rt3_b2[tp_]
                        xg3_b = xg3_b2[tp_]
                        ylr = ("yl", tp_)
                        rtr = ("rt3", tp_)
                        xgr = ("xg3", tp_)
                        p.dma("sp", yl_b,
                              W["sp_y"][r0:r0 + TT3A, :]
                              .rearrange("(b q) f -> q b f", q=128),
                              writes=(ylr,), tag=f"ldy{t}")
                        for i in range(2):
                            tt = 2 * t + i
                            p.dma("sp", rt3_b[:, :, 256 * i:256 * (i + 1)],
                                  W["sp_r"][tt].rearrange("c p n -> p c n"),
                                  writes=(rtr,))
                            p.dma("act", xg3_b[:, :, 256 * i:256 * (i + 1)],
                                  W["sp_xg"][tt].rearrange("c p n -> p c n"),
                                  writes=(xgr,))
                        for bi in range(NB3A):
                            pyp = psY3[bi % 2]
                            pyres = ("psY31", "psY32")[bi % 2]
                            def gmm(e, bi=bi, pyp=pyp, xg3_b=xg3_b):
                                last = None
                                for cc in range(8):
                                    for nn in range(2):
                                        last = e.matmul(
                                            pyp[:, 512 * nn:512 * (nn + 1)],
                                            xg3_b[:, cc, 128 * bi:128 * (bi + 1)],
                                            wgT_b3[:, cc, 512 * nn:512 * (nn + 1)],
                                            start=(cc == 0), stop=(cc == 7))
                                return last
                            p.op("pe", gmm, reads=(xgr, "wg"), writes=(pyres,))
                            p.op("act", lambda e, bi=bi, pyp=pyp: e.activation(
                                g3_tm[:, bi, :], pyp[:], AF.Silu),
                                reads=(pyres,), writes=("g3",))
                        if t == 0:
                            p.op("dve", lambda e: e.memset(ystg3[:], 0.0),
                                 writes=("ystg3",))
                        def gn_block(bi):
                            off = 16 * bi
                            ys = yfull[:, bi, :]
                            su = st3a[:, off:off + 16]
                            sq = st3a[:, 64 + off:64 + off + 16]
                            yv = ys.rearrange("p (h n) -> p h n", h=H)
                            p.op("dve", lambda e, yv=yv, su=su: e.reduce_sum(
                                su.unsqueeze(2), yv, axis=AX.X),
                                reads=("yf",), writes=("st3",))
                            p.op("act", lambda e, ys=ys: e.activation(
                                sq3_s[:], ys, AF.Square),
                                reads=("yf",), writes=("sq3",))
                            p.op("dve", lambda e, sq=sq: e.reduce_sum(
                                sq.unsqueeze(2),
                                sq3_s.rearrange("p (h n) -> p h n", h=H), axis=AX.X),
                                reads=("sq3",), writes=("st3",))
                            p.op("dve", lambda e, su=su: e.tensor_scalar(
                                su, su, 1.0 / HS, None, OP.mult),
                                reads=("st3",), writes=("st3",))
                            p.op("dve", lambda e, su=su, sq=sq: e.tensor_scalar(
                                sq, sq, 1.0 / HS, None, OP.mult),
                                reads=("st3",), writes=("st3",))
                            m2 = st3a[:, 128 + off:128 + off + 16]
                            p.op("dve", lambda e, su=su, m2=m2: e.tensor_mul(m2, su, su),
                                 reads=("st3",), writes=("st3",))
                            p.op("dve", lambda e, sq=sq, m2=m2: e.tensor_sub(sq, m2, sq),
                                 reads=("st3",), writes=("st3",))
                            p.op("act", lambda e, sq=sq: e.activation(
                                sq, sq, AF.Sqrt, bias=EPSG, scale=NEGC),
                                reads=("st3",), writes=("st3",))
                            p.op("dve", lambda e, sq=sq: e.reciprocal(sq, sq),
                                 reads=("st3",), writes=("st3",))
                            def gnn(e, ys=ys, su=su, sq=sq):
                                last = None
                                for h in range(H):
                                    last = e.tensor_scalar(
                                        zt3[:, 64 * h:64 * h + 64],
                                        ys[:, 64 * h:64 * h + 64],
                                        su[:, h:h + 1], sq[:, h:h + 1],
                                        OP.subtract, OP.mult)
                                return last
                            p.op("dve", gnn, reads=("yf", "st3"), writes=("zt3",))
                            p.op("dve", lambda e: e.tensor_mul(
                                gp3[:], zt3[:], gam_c[:]),
                                reads=("zt3", "gam"), writes=("gp",))
                            p.op("dve", lambda e: e.tensor_add(
                                gp3[:], gp3[:], bet_c[:]),
                                reads=("gp", "bet"), writes=("gp",))
                            p.op("dve", lambda e, bi=bi: e.tensor_mul(
                                z3_b[:, bi, :], gp3[:], g3_tm[:, bi, :]),
                                reads=("gp", "g3"), writes=("z3",))
                            def ztp(e, bi=bi):
                                last = None
                                with nc.allow_low_precision(reason="bf16 transpose"):
                                    for cc in range(8):
                                        last = e.matmul(
                                            psT3[:, 128 * cc:128 * (cc + 1)],
                                            z3_b[:, bi, 128 * cc:128 * (cc + 1)],
                                            i128b_c, start=True, stop=True)
                                return last
                            p.op("pe", ztp, reads=("z3", "i128b"), writes=("psT3",))
                            p.op("dve", lambda e, bi=bi: e.tensor_copy(
                                zf3_b[:, :, 128 * bi:128 * (bi + 1)],
                                psT3.rearrange("p (c n) -> p c n", c=8)),
                                reads=("psT3",), writes=("zf3",))
                        for m in range(TT3A // L):
                            gm = (r0 // L) + m
                            bm = m // 2
                            mp = m % 2
                            pyp = psY3[bm % 2]
                            pyo = 512 * mp
                            pyres = ("psY31", "psY32")[bm % 2]
                            p.op("dve", lambda e, gm=gm: e.tensor_mul(
                                s0m_b[:, :, :], s0_f[:, :, :],
                                dm_fm[:, :, gm:gm + 1]
                                .broadcast_to([128, 8, HS])),
                                reads=("s0", "dm"), writes=("s0m",))
                            def ycmm(e, m=m, pyp=pyp, pyo=pyo, rt3_b=rt3_b):
                                last = None
                                for h in range(H):
                                    ch, hh = h // 2, (h % 2) * 64
                                    last = e.matmul(
                                        pyp[hh:hh + 64, pyo + 64 * ch:pyo + 64 * ch + 64],
                                        rt3_b[hh:hh + 64, ch, L * m:L * (m + 1)],
                                        s0m_b[hh:hh + 64, ch, :],
                                        start=True, stop=True)
                                return last
                            p.op("pe", ycmm, reads=(rtr, "s0m"), writes=(pyres,))
                            mo = 64 * mp
                            def yf1(e, pyp=pyp, pyo=pyo, mo=mo, bm=bm, mp=mp, yl_b=yl_b):
                                yv = yfull[mo:mo + 64, bm, :].rearrange(
                                    "p (c two l) -> p c two l", two=2, l=HS)[:, :, mp, :]
                                lv = yl_b[mo:mo + 64, bm, :].rearrange(
                                    "p (c two l) -> p c two l", two=2, l=HS)[:, :, mp, :]
                                return e.tensor_add(
                                    yv,
                                    pyp[mo:mo + 64, pyo:pyo + 512]
                                    .rearrange("p (c l) -> p c l", l=HS), lv)
                            p.op("dve", yf1, reads=(pyres, ylr), writes=("yf",))
                            def yst3(e, pyp=pyp, pyo=pyo, mp=mp):
                                so = 64 * (1 - mp)
                                return e.tensor_copy(ystg3[so:so + 64, :],
                                                     pyp[so:so + 64, pyo:pyo + 512])
                            p.op("dve", yst3, reads=(pyres,), writes=("ystg3",))
                            p.op("pe", lambda e: e.matmul(psS3[:], selsw_c, ystg3[:],
                                                          start=True, stop=True),
                                 reads=("ystg3", "selsw"), writes=("psS3",))
                            def yf2(e, mo=mo, bm=bm, mp=mp, yl_b=yl_b):
                                yv = yfull[mo:mo + 64, bm, :].rearrange(
                                    "p (c two l) -> p c two l", two=2, l=HS)[:, :, 1 - mp, :]
                                lv = yl_b[mo:mo + 64, bm, :].rearrange(
                                    "p (c two l) -> p c two l", two=2, l=HS)[:, :, 1 - mp, :]
                                return e.tensor_add(
                                    yv,
                                    psS3[mo:mo + 64, 0:512]
                                    .rearrange("p (c l) -> p c l", l=HS), lv)
                            p.op("dve", yf2, reads=("psS3", ylr), writes=("yf",))
                            if m % 2 == 1:
                                gn_block(m // 2)
                        for bi in range(NB3A):
                            pyp = psY3[bi % 2]
                            pyres = ("psY31", "psY32")[bi % 2]
                            def aomm(e, bi=bi, pyp=pyp):
                                last = None
                                for cc in range(8):
                                    for nn in range(2):
                                        last = e.matmul(
                                            pyp[:, 512 * nn:512 * (nn + 1)],
                                            zf3_b[:, cc, 128 * bi:128 * (bi + 1)],
                                            woT_b3[:, cc, 512 * nn:512 * (nn + 1)],
                                            start=(cc == 0), stop=(cc == 7))
                                return last
                            p.op("pe", aomm, reads=("zf3", "wo"), writes=(pyres,))
                            cl = st3a[:, 120:128]
                            p.op("act", lambda e, pyp=pyp, cl=cl: e.activation(
                                ao3[:], pyp[:], AF.Copy, accum_out=cl[:, 0:1]),
                                reads=(pyres,), writes=("ao", "stl2"))
                            p.op("act", lambda e, cl=cl: e.activation(
                                sq3_s[:], ao3[:], AF.Square, accum_out=cl[:, 1:2]),
                                reads=("ao",), writes=("sq3", "stl2"))
                            p.op("dve", lambda e, cl=cl: e.tensor_scalar(
                                cl[:, 2:3], cl[:, 0:1], 1.0 / C, None, OP.mult),
                                reads=("stl2",), writes=("stl2",))
                            p.op("dve", lambda e, cl=cl: e.tensor_scalar(
                                cl[:, 3:4], cl[:, 1:2], 1.0 / C, None, OP.mult),
                                reads=("stl2",), writes=("stl2",))
                            p.op("dve", lambda e, cl=cl: e.scalar_tensor_tensor(
                                cl[:, 3:4], cl[:, 2:3], cl[:, 2:3], cl[:, 3:4],
                                OP.mult, OP.subtract),
                                reads=("stl2",), writes=("stl2",))
                            p.op("act", lambda e, cl=cl: e.activation(
                                cl[:, 4:5], cl[:, 3:4], AF.Sqrt, bias=EPS1,
                                scale=NEGC),
                                reads=("stl2",), writes=("stl2",))
                            p.op("dve", lambda e, cl=cl: e.reciprocal(
                                cl[:, 4:5], cl[:, 4:5]),
                                reads=("stl2",), writes=("stl2",))
                            p.op("dve", lambda e, cl=cl: e.tensor_scalar(
                                cl[:, 5:6], cl[:, 2:3], cl[:, 4:5], -1.0,
                                OP.mult, OP.mult),
                                reads=("stl2",), writes=("stl2",))
                            p.op("act", lambda e, bi=bi, cl=cl: e.activation(
                                l2_b[:, bi, :], ao3[:], AF.Identity, bias=cl[:, 5:6],
                                scale=cl[:, 4:5]),
                                reads=("ao", "stl2"), writes=("l2",))
                        p.dma("pool", W["sp_l2"][r0:r0 + TT3A, :]
                              .rearrange("(b q) f -> q b f", q=128), l2_b[:],
                              reads=("l2",), writes=(), tag="spl2")
                        if t == DBG_TILE3A:
                            for nm, (src, res) in {
                                    "g3": (g3_tm, "g3"), "yfull": (yfull, "yf"),
                                    "z3": (z3_b, "z3"), "l2": (l2_b, "l2"),
                                    "s0": (s0_f, "s0")}.items():
                                da = dbg_out(nm, list(src.shape), src.dtype)
                                p.dma("sp", da, src, reads=(res,), tag="dbg")

                if "3" in PHASES:
                    sems = run_phase(nc, sems, dsem, phase3a, "p3a")

        with ExitStack() as st4:
            sb4 = lambda n, sh, dt=F32: st4.enter_context(nc.sbuf_tensor(n, list(sh), dt)).ap()
            ps4 = lambda n, sh, dt=F32: st4.enter_context(nc.psum_tensor(n, list(sh), dt)).ap()
            wiT_b = sb4("wiT_b", (128, 8, FF), BF16)
            woFT_b = sb4("woFT_b", (128, 32, C), BF16)
            wgateT_b = sb4("wgateT_b", (128, 8, E), BF16)
            bg_c = sb4("bg_c", (128, E))
            aAllT_b = sb4("aAllT_b", (128, 8, 128), BF16)
            bt8P = [sb4(f"bt8P_{pr}", (128, FF), BF16) for pr in range(2)]
            NB3B = TT3B // 128
            x32 = [sb4(f"x3_{i}", (128, NB3B, C)) for i in range(2)]
            l232 = [sb4(f"l23_{i}", (128, NB3B, C), BF16) for i in range(2)]
            x3n_tm = sb4("x3n_tm", (128, NB3B, C), BF16)
            x3nf2 = [sb4(f"x3nf_{i}", (128, 8, TT3B), BF16) for i in range(2)]
            sq4_s = sb4("sq4_s", (128, C), BF16)
            sq4b = sb4("sq4b", (128, C), BF16)
            st4a = sb4("st4a", (128, 32))
            lg4 = sb4("lg4", (128, NB3B, E))
            wt42 = [sb4(f"wt4_{i}", (128, NB3B, E)) for i in range(2)]
            lle8 = [[sb4(f"lle8_{e}_{i}", (128, TT3B), BF16) for e in range(E)]
                    for i in range(2)]
            lle_stg = sb4("lle_stg", (4, TT3B), BF16)
            base_b = sb4("base_b", (128, 2, 512), BF16)
            u4p = [[sb4(f"u4_{fp}_{e}", (128, 512), BF16) for e in range(E)]
                   for fp in range(2)]
            hcT_b = sb4("hcT_b", (128, 2, 512), BF16)
            ffn_f = sb4("ffn_f", (128, C))
            psA4 = ps4("psA4", (128, 512))
            psB4 = [ps4("psB4a", (128, 512)), ps4("psB4b", (128, 512))]
            psS4 = ps4("psS4", (128, 512))
            _psY41 = ps4("psY41", (128, 1024))
            psY4 = [_psY41, _psY41]
            psT4 = ps4("psT4", (128, 512))
            psT4x = ps4("psT4x", (128, 512))

            def phase3b(p):
                p.dma("pool", wiT_b, W["wiT"].rearrange("(c p) n -> p c n", p=128),
                      writes=("wi",))
                p.dma("pool", woFT_b, W["woFT"].rearrange("(c p) n -> p c n", p=128),
                      writes=("wof",))
                p.dma("pool", wgateT_b, W["wgateT"].rearrange("(c p) n -> p c n", p=128),
                      writes=("wgt",))
                p.dma("sp", bg_c, W["bg_bc"], writes=("bg",))
                p.dma("pool", aAllT_b, W["aAllT128"].rearrange("(c p) n -> p c n", p=128),
                      writes=("aA",))
                for pr in range(2):
                    p.dma("pool", bt8P[pr], W[f"bT8P{pr}"], writes=(f"bt{pr}",))
                for i_ in range(2):
                    for e_ in range(E):
                        p.op("dve", lambda e, i_=i_, e_=e_: e.memset(
                            lle8[i_][e_][:], 0.0),
                            writes=((f"lle{e_}", i_),))

                def front_dma(t):
                    tp = t % 2
                    r0 = t * TT3B
                    p.dma("sp", x32[tp], W["xs"][128 + r0:128 + r0 + TT3B, :]
                          .rearrange("(b q) f -> q b f", q=128),
                          writes=(("x3", tp),), tag=f"lx{t}")
                    p.dma("sp", l232[tp], W["sp_l2"][r0:r0 + TT3B, :]
                          .rearrange("(b q) f -> q b f", q=128),
                          writes=(("l23", tp),))

                def front_norm(t):
                    tp = t % 2
                    x2f = x32[tp]
                    xres = ("x3", tp)
                    p.op("dve", lambda e, x2f=x2f, tp=tp: e.tensor_add(
                        x2f[:], x2f[:], l232[tp][:]),
                        reads=(xres, ("l23", tp)), writes=(xres,))
                    rstd, mb, st = ln_stats(p, st4a, 8 * tp, x2f[:, 0, :],
                                            sq4b[:], (xres,), C, EPS1, NEGC,
                                            tag=f"l3{t}")
                    p.op("act", lambda e, x2f=x2f, rstd=rstd, mb=mb: e.activation(
                        x3n_tm[:, 0, :], x2f[:, 0, :], AF.Identity, bias=mb,
                        scale=rstd),
                        reads=(xres, st), writes=("x3n",))

                def front_xtp(t):
                    tp = t % 2
                    xfr = ("x3nf", tp)
                    for q4 in range(2):
                        def xtp(e, q4=q4):
                            last = None
                            with nc.allow_low_precision(reason="bf16 transpose"):
                                for cc in range(4 * q4, 4 * q4 + 4):
                                    last = e.matmul(
                                        psT4x[:, 128 * (cc % 4):128 * (cc % 4 + 1)],
                                        x3n_tm[:, 0, 128 * cc:128 * (cc + 1)],
                                        i128b_c, start=True, stop=True)
                            return last
                        p.op("pe", xtp, reads=("x3n", "i128b"), writes=("psT4x",))
                        p.op("dve", lambda e, q4=q4, tp=tp: e.tensor_copy(
                            x3nf2[tp][:, 4 * q4:4 * q4 + 4, :],
                            psT4x.rearrange("p (c n) -> p c n", c=4)),
                            reads=("psT4x",), writes=(xfr,))

                def gate_a(t):
                    tp = t % 2
                    xf = x3nf2[tp]
                    xfr = ("x3nf", tp)
                    wt4 = wt42[tp]
                    wtres = ("wt", tp)

                    def lgm(e, xf=xf):
                        last = None
                        for cc in range(8):
                            last = e.matmul(psS4[:, 0:4], xf[:, cc, :],
                                            wgateT_b[:, cc, :],
                                            start=(cc == 0), stop=(cc == 7))
                        return last
                    p.op("pe", lgm, reads=(xfr, "wgt"), writes=("psS4",))
                    p.op("dve", lambda e: e.tensor_add(
                        lg4[:], psS4[:, 0:4 * NB3B].rearrange("p (b e) -> p b e",
                                                              b=NB3B),
                        bg_c[:, :].unsqueeze(1).broadcast_to([128, NB3B, E])),
                        reads=("psS4", "bg"), writes=("lg",))
                    sc = lg4[:, 0, :]
                    mx = st4a[:, 16:17]
                    sm = st4a[:, 17:18]
                    p.op("dve", lambda e, sc=sc, mx=mx: e.reduce_max(
                        mx, sc, axis=AX.X, negate=True),
                        reads=("lg",), writes=("st4",))
                    p.op("act", lambda e, sc=sc, mx=mx: e.activation(
                        sc, sc, AF.Exp, bias=mx),
                        reads=("lg", "st4"), writes=("lg",))
                    p.op("dve", lambda e, sc=sc, sm=sm: e.reduce_sum(
                        sm, sc, axis=AX.X),
                        reads=("lg",), writes=("st4",))
                    p.op("dve", lambda e, sm=sm: e.reciprocal(sm, sm),
                         reads=("st4",), writes=("st4",))
                    p.op("dve", lambda e, sc=sc, sm=sm: e.tensor_scalar(
                        sc, sc, sm, None, OP.mult),
                        reads=("lg", "st4"), writes=("lg",))
                    a_, b_, c_, d_ = (sc[:, 0:1], sc[:, 1:2], sc[:, 2:3], sc[:, 3:4])
                    t0 = st4a[:, 18:19]; t1 = st4a[:, 19:20]
                    t2 = st4a[:, 20:21]; t3 = st4a[:, 21:22]
                    sec = st4a[:, 22:23]
                    for (dst, xx, yy, op) in ((t0, a_, b_, OP.max),
                                              (t1, a_, b_, OP.min),
                                              (t2, c_, d_, OP.max),
                                              (t3, c_, d_, OP.min)):
                        p.op("dve", lambda e, dst=dst, xx=xx, yy=yy, op=op:
                             e.tensor_tensor(dst, xx, yy, op),
                             reads=("lg", "st4"), writes=("st4",))
                    p.op("dve", lambda e: e.tensor_tensor(t0, t0, t2, OP.min),
                         reads=("st4",), writes=("st4",))
                    p.op("dve", lambda e: e.tensor_max(t1, t1, t3),
                         reads=("st4",), writes=("st4",))
                    p.op("dve", lambda e: e.tensor_max(sec, t0, t1),
                         reads=("st4",), writes=("st4",))
                    p.op("dve", lambda e, sc=sc, wt4=wt4: e.tensor_scalar(
                        wt4[:, 0, :], sc, sec, None, OP.subtract),
                        reads=("lg", "st4"), writes=(wtres,))
                    p.op("act", lambda e, wt4=wt4: e.activation(
                        wt4[:, 0, :], wt4[:, 0, :], AF.Sign),
                        reads=(wtres,), writes=(wtres,))
                    p.op("dve", lambda e, wt4=wt4: e.tensor_scalar(
                        wt4[:, 0, :], wt4[:, 0, :], 1.0, 1.0, OP.add, OP.min),
                        reads=(wtres,), writes=(wtres,))
                    p.op("dve", lambda e, sc=sc, wt4=wt4: e.tensor_mul(
                        wt4[:, 0, :], wt4[:, 0, :], sc),
                        reads=(wtres, "lg"), writes=(wtres,))

                def gate_b(t):
                    tp = t % 2
                    xf = x3nf2[tp]
                    xfr = ("x3nf", tp)
                    lle = lle8[tp]

                    def llm(e, xf=xf):
                        last = None
                        for e_ in range(E):
                            for cc in range(8):
                                last = e.matmul(psS4[0:4, TT3B * e_:TT3B * (e_ + 1)],
                                                aAllT_b[:, cc, 32 * e_:32 * e_ + 4],
                                                xf[:, cc, :],
                                                start=(cc == 0), stop=(cc == 7))
                        return last
                    p.op("pe", llm, reads=(xfr, "aA"), writes=("psS4",))
                    for e_ in range(E):
                        if e_ % 2 == 0:
                            p.op("act", lambda e, e_=e_, lle=lle: e.activation(
                                lle[e_][0:4, :],
                                psS4[0:4, TT3B * e_:TT3B * (e_ + 1)], AF.Copy),
                                reads=("psS4",), writes=((f"lle{e_}", tp),))
                        else:
                            p.op("act", lambda e, e_=e_: e.activation(
                                lle_stg[:],
                                psS4[0:4, TT3B * e_:TT3B * (e_ + 1)], AF.Copy),
                                reads=("psS4",), writes=("lstg",))
                            p.dma("act", lle[e_][4:8, :], lle_stg[:],
                                  reads=("lstg",), writes=((f"lle{e_}", tp),))

                def back(t):
                    tp = t % 2
                    r0 = t * TT3B
                    xres = ("x3", tp)
                    xfr = ("x3nf", tp)
                    xf = x3nf2[tp]
                    x2f = x32[tp]
                    wt4 = wt42[tp]
                    wtres = ("wt", tp)
                    lle = lle8[tp]
                    if t + 1 < NT3B:
                        front_dma(t + 1)
                    pyp = psY4[0]
                    pyres = "psY41"

                    def emit_bmm(fc):
                        def bmm(e, fc=fc, xf=xf):
                            last = None
                            for cc in range(8):
                                last = e.matmul(
                                    psA4[:], xf[:, cc, :],
                                    wiT_b[:, cc, 512 * fc:512 * (fc + 1)],
                                    start=(cc == 0), stop=(cc == 7))
                            return last
                        p.op("pe", bmm, reads=(xfr, "wi"), writes=("psA4",))
                        p.op("act", lambda e, fp=fc % 2: e.activation(
                            base_b[:, fp, :], psA4[:], AF.Copy),
                            reads=("psA4",), writes=(("baseb", fc % 2),))

                    def emit_emm_ts(fc):
                        fp = fc % 2
                        for ei in range(E):
                            ep = ei % 2
                            pbx = psB4[ep]
                            pbres = ("psB4a", "psB4b")[ep]
                            def emm(e, fc=fc, ei=ei, fp=fp, pbx=pbx, lle=lle):
                                e.matmul(pbx[:], lle[ei][:, :],
                                         bt8P[ei // 2][:, 512 * fc:512 * (fc + 1)],
                                         start=True, stop=False)
                                return e.matmul(pbx[:], i128b_c, base_b[:, fp, :],
                                                start=False, stop=True)
                            p.op("pe", emm,
                                 reads=(("baseb", fp), (f"lle{ei}", tp),
                                        f"bt{ei // 2}", "i128b"),
                                 writes=(pbres,))
                            ub = u4p[fp][ei]
                            ubres = (f"u4_{ei}", fp)
                            if ei in (0, 2):
                                p.op("dve", lambda e, ei=ei, pbx=pbx, wt4=wt4,
                                     ub=ub:
                                     e.tensor_scalar(ub[:], pbx[:],
                                                     wt4[:, 0, ei:ei + 1], 0.0,
                                                     OP.mult, OP.max),
                                     reads=(pbres, wtres), writes=(ubres,))
                            else:
                                p.op("act", lambda e, ei=ei, pbx=pbx, wt4=wt4,
                                     ub=ub:
                                     e.activation(ub[:], pbx[:], AF.Relu,
                                                  scale=wt4[:, 0, ei:ei + 1]),
                                     reads=(pbres, wtres), writes=(ubres,))

                    def emit_hct(fc):
                        fp = fc % 2
                        def hct(e, fp=fp):
                            last = None
                            with nc.allow_low_precision(reason="bf16 transpose"):
                                for e_ in range(E):
                                    for j in range(4):
                                        last = e.matmul(
                                            psT4[:, 128 * j:128 * (j + 1)],
                                            u4p[fp][e_][:, 128 * j:128 * (j + 1)],
                                            i128b_c,
                                            start=(e_ == 0 and j == 0),
                                            stop=(e_ == 3 and j == 3))
                            return last
                        p.op("pe", hct,
                             reads=tuple((f"u4_{e_}", fp) for e_ in range(E))
                             + ("i128b",),
                             writes=("psT4",))
                        p.op("dve", lambda e, fp=fp: e.tensor_copy(
                            hcT_b[:, fp, :], psT4[:]),
                            reads=("psT4",), writes=(("hcT", fp),))

                    def emit_omm(fc):
                        fp = fc % 2
                        def omm(e, fc=fc, pyp=pyp, fp=fp):
                            last = None
                            for j in range(4):
                                for nn in range(2):
                                    last = e.matmul(
                                        pyp[:, 512 * nn:512 * (nn + 1)],
                                        hcT_b[:, fp, 128 * j:128 * (j + 1)],
                                        woFT_b[:, 4 * fc + j,
                                               512 * nn:512 * (nn + 1)],
                                        start=(fc == 0 and j == 0),
                                        stop=(fc == 7 and j == 3))
                            return last
                        p.op("pe", omm, reads=(("hcT", fp), "wof"),
                             writes=(pyres,))

                    emit_bmm(0)
                    emit_emm_ts(0)
                    for fc in range(8):
                        if fc < 7:
                            emit_bmm(fc + 1)
                        emit_hct(fc)
                        if fc < 7:
                            emit_emm_ts(fc + 1)
                        emit_omm(fc)
                        if t + 1 < NT3B:
                            if fc == 1:
                                front_norm(t + 1)
                            elif fc == 3:
                                front_xtp(t + 1)
                            elif fc == 5:
                                gate_a(t + 1)
                            elif fc == 6:
                                gate_b(t + 1)
                    cl = st4a[:, 24:32]
                    p.op("act", lambda e, pyp=pyp, cl=cl: e.activation(
                        ffn_f[:], pyp[:], AF.Copy, accum_out=cl[:, 0:1]),
                        reads=(pyres,), writes=("ffn", "stl4"))
                    p.op("act", lambda e, cl=cl: e.activation(
                        sq4_s[:], ffn_f[:], AF.Square, accum_out=cl[:, 1:2]),
                        reads=("ffn",), writes=("sq4", "stl4"))
                    p.op("dve", lambda e, cl=cl: e.tensor_scalar(
                        cl[:, 2:3], cl[:, 0:1], 1.0 / C, None, OP.mult),
                        reads=("stl4",), writes=("stl4",))
                    p.op("dve", lambda e, cl=cl: e.tensor_scalar(
                        cl[:, 3:4], cl[:, 1:2], 1.0 / C, None, OP.mult),
                        reads=("stl4",), writes=("stl4",))
                    p.op("dve", lambda e, cl=cl: e.scalar_tensor_tensor(
                        cl[:, 3:4], cl[:, 2:3], cl[:, 2:3], cl[:, 3:4],
                        OP.mult, OP.subtract),
                        reads=("stl4",), writes=("stl4",))
                    p.op("act", lambda e, cl=cl: e.activation(
                        cl[:, 4:5], cl[:, 3:4], AF.Sqrt, bias=EPS1, scale=NEGC),
                        reads=("stl4",), writes=("stl4",))
                    p.op("dve", lambda e, cl=cl: e.reciprocal(cl[:, 4:5],
                                                              cl[:, 4:5]),
                         reads=("stl4",), writes=("stl4",))
                    p.op("dve", lambda e, cl=cl: e.tensor_scalar(
                        cl[:, 5:6], cl[:, 2:3], cl[:, 4:5], -1.0,
                        OP.mult, OP.mult),
                        reads=("stl4",), writes=("stl4",))
                    p.op("dve", lambda e, cl=cl, x2f=x2f: e.scalar_tensor_tensor(
                        x2f[:, 0, :], ffn_f[:], cl[:, 4:5], x2f[:, 0, :],
                        OP.mult, OP.add),
                        reads=("ffn", "stl4", xres), writes=(xres,))
                    p.op("dve", lambda e, cl=cl, x2f=x2f: e.tensor_scalar(
                        x2f[:, 0, :], x2f[:, 0, :], cl[:, 5:6], None,
                        OP.add),
                        reads=(xres, "stl4"), writes=(xres,))
                    p.dma("sp", out_ap[r0:r0 + TT3B, :]
                          .rearrange("(b q) f -> q b f", q=128),
                          x2f[:], reads=(xres,), writes=("outdma",), tag="out")

                front_dma(0)
                front_norm(0)
                front_xtp(0)
                gate_a(0)
                gate_b(0)
                for t in range(NT3B):
                    back(t)

            if "4" in PHASES:
                sems = run_phase(nc, sems, dsem, phase3b, "p3b")

    return nc, dbg_names


def _prep_inputs(inp):
    x = np.asarray(inp["L0_attn_x"])
    A_lora = np.asarray(inp["L0_attn_A_lora"])
    B_lora = np.asarray(inp["L0_attn_B_lora"])
    lam = np.asarray(inp["L0_attn_lambda_"]).reshape(5, C)
    shared = {
        "wkT": np.ascontiguousarray(np.asarray(inp["L0_attn_Wk"]).T),
        "wrT": np.ascontiguousarray(np.asarray(inp["L0_attn_Wr"]).T),
        "wvT": np.ascontiguousarray(np.asarray(inp["L0_attn_Wv"]).T),
        "wgT": np.ascontiguousarray(np.asarray(inp["L0_attn_Wg"]).T),
        "woT": np.ascontiguousarray(np.asarray(inp["L0_attn_Wo"]).T),
        "alora": np.ascontiguousarray(A_lora),
        "blora33": np.ascontiguousarray(
            np.concatenate([B_lora, lam[:, None, :]], axis=1)),
        "tdA": np.ascontiguousarray(np.asarray(inp["L0_attn_td_A"])),
        "tdB65": np.ascontiguousarray(np.concatenate(
            [np.asarray(inp["L0_attn_td_B"]),
             np.asarray(inp["L0_attn_td_miu"]).reshape(1, C)], axis=0)),
        "miux_col": np.ascontiguousarray(
            np.asarray(inp["L0_attn_miu_x"]).reshape(8, 128).T),
        "u_col": np.ascontiguousarray(
            np.asarray(inp["L0_attn_u"]).reshape(8, 128).T),
        "gam_bc": np.ascontiguousarray(np.broadcast_to(
            np.asarray(inp["L0_attn_gn_gamma"]).reshape(1, C), (128, C))),
        "bet_bc": np.ascontiguousarray(np.broadcast_to(
            np.asarray(inp["L0_attn_gn_beta"]).reshape(1, C), (128, C))),
        "maskU2": np.ascontiguousarray(np.tile(
            np.triu(np.ones((L, L), np.float32), 1), (2, 1))),
        "i128f": np.eye(128, dtype=np.float32),
        "i128b": np.eye(128, dtype=BFD),
        "selswap": np.ascontiguousarray(np.roll(np.eye(128, dtype=BFD), 64, axis=1)),
        "wiT": np.ascontiguousarray(np.asarray(inp["L0_ffn_Wi"]).T),
        "woFT": np.ascontiguousarray(np.asarray(inp["L0_ffn_Wo"]).T),
        "wgateT": np.ascontiguousarray(np.asarray(inp["L0_ffn_Wg"]).T),
        "bg_bc": np.ascontiguousarray(np.broadcast_to(
            np.asarray(inp["L0_ffn_bg"]).reshape(1, E), (128, E))),
    }
    hind = np.zeros((8, 128, 16), BFD)
    for cc in range(8):
        for pp in range(128):
            hind[cc, pp, (128 * cc + pp) // HS] = 1
    shared["hind8"] = hind
    cvec = np.zeros((128, 4), np.float32)
    cvec[:, 0] = -1.0
    cvec[:, 1] = 1e-5
    cvec[:, 2] = 1e-5 * H
    shared["cvec"] = cvec
    aT = np.zeros((C, 128), np.float32)
    Am = np.asarray(inp["L0_ffn_A"])
    for e in range(E):
        aT[:, 32 * e:32 * e + 4] = Am[e].T
    shared["aAllT128"] = aT
    Bm = np.asarray(inp["L0_ffn_B"])
    for e in range(E):
        pass
    for pr in range(2):
        btp = np.zeros((128, FF), np.float32)
        btp[0:4] = Bm[2 * pr].T
        btp[4:8] = Bm[2 * pr + 1].T
        shared[f"bT8P{pr}"] = btp

    per_core = []
    for c in range(NCORE):
        bb, ppos = c // 4, c % 4
        s0 = ppos * SLAB
        xs = np.zeros((128 + SLAB, C), np.float32)
        xs[128:] = x[bb, s0:s0 + SLAB]
        if ppos > 0:
            xs[:128] = x[bb, s0 - 128:s0]
        pm = np.zeros((128, 8), np.float32)
        for q in range(8):
            if q // 4 == bb and q % 4 < ppos:
                pm[:, q] = 1.0
        m = dict(shared)
        m["xs"] = xs
        m["prevmask"] = pm
        per_core.append(m)
    return per_core


def kernel(**inp):
    per_core = _prep_inputs(inp)
    nc, dbg_names = build_nc()
    res = run_bass_kernel_spmd(nc, per_core, core_ids=list(range(NCORE)),
                               trace=TRACE_RUN)
    out = np.zeros((B, T, C), np.float32)
    for c in range(NCORE):
        bb, ppos = c // 4, c % 4
        if "out" in res.results[c]:
            out[bb, ppos * SLAB:(ppos + 1) * SLAB] = res.results[c]["out"]
    kernel._last = res
    return out


# version 25 (final)
# speedup vs baseline: 1.0198x; 1.0198x over previous
import numpy as np
import ml_dtypes
from contextlib import ExitStack
import concourse.bass as bass
import concourse.mybir as mybir
from concourse.bass_utils import run_bass_kernel_spmd

F32 = mybir.dt.float32
BF16 = mybir.dt.bfloat16
AF = mybir.ActivationFunctionType
OP = mybir.AluOpType
AX = mybir.AxisListType
BFD = ml_dtypes.bfloat16

B, T, C = 2, 16384, 1024
H, HS = 16, 64
NCORE = 8
SLAB = T // 4
L = 64
TT1 = 256
NT1 = SLAB // TT1
CH1 = TT1 // L
NB1 = TT1 // 128
TT3A = 512
NT3A = SLAB // TT3A
NB3A = TT3A // 128
TT3B = 128
NT3B = SLAB // TT3B
NCH = SLAB // L
E, FF = 4, 4096

DBG_TILE = -1
DBG_TILE3A = -1
PHASES = "1234"
TILE_LIM = None
STAGE_LIM = 99
SAFE_CHAIN = True
RACE_DET = True
S1_NO = set()
OP_LIM = 10 ** 9
OP_SKIP = set()
SKIP_ENG = set()
TRACE_RUN = False


class Prog:
    CENG = ("pe", "dve", "act", "pool", "sp")
    DMAQ = ("sp", "act", "pool")

    def __init__(self, nc, sems, dsem):
        self.nc = nc
        self.sems = sems
        self.dsem = dsem
        self.ops = []
        self.last_write = {}
        self.readers = {}

    def op(self, engine, fn, reads=(), writes=(), dma=False, tag="", skey=None,
           step=None):
        if len(self.ops) >= OP_LIM:
            return -1
        if getattr(self, "_rawidx", None) is None:
            self._rawidx = 0
        self._rawidx += 1
        if (self._rawidx - 1) in OP_SKIP:
            return -1
        if not dma and engine in SKIP_ENG:
            return -1
        psr = tuple(r for r in reads
                    if isinstance(r, str) and r.startswith("ps") and r not in writes)
        if psr:
            writes = tuple(writes) + psr
        opid = len(self.ops)
        deps = set()
        for r in reads:
            if r in self.last_write:
                deps.add(self.last_write[r])
        for w in writes:
            if w in self.last_write:
                deps.add(self.last_write[w])
            for rd in self.readers.get(w, ()):
                deps.add(rd)
        deps.discard(opid)
        if dma and skey is None:
            skey = ("dma", engine) + tuple(writes[:1] if writes else reads[:1])
        self.ops.append(dict(id=opid, eng=engine, fn=fn, deps=deps, dma=dma,
                             tag=tag, needed=False, skey=skey,
                             step=(step if step is not None else (16 if dma else 1))))
        for r in reads:
            self.readers.setdefault(r, []).append(opid)
        for w in writes:
            self.last_write[w] = opid
            self.readers[w] = []
        return opid

    def dma(self, queue, out, in_, reads=(), writes=(), tag="", skey=None):
        return self.op(queue, lambda e, o=out, i=in_: e.dma_start(out=o, in_=i),
                       reads=reads, writes=writes, dma=True, tag=tag, skey=skey)

    def prepare(self):
        for o in self.ops:
            if o["dma"]:
                o["needed"] = True
                if o["skey"] not in self.dsem:
                    pool = self.dsem["__pool__"]
                    idx = len(self.dsem) - 1
                    assert idx < len(pool), "dma sem pool exhausted"
                    self.dsem[o["skey"]] = [pool[idx], 0]
        for o in self.ops:
            for d in o["deps"]:
                pp = self.ops[d]
                if SAFE_CHAIN or pp["dma"] or pp["eng"] != o["eng"]:
                    pp["needed"] = True
        cum = {}
        for o in self.ops:
            key = o["skey"] if o["dma"] else o["eng"]
            cum[key] = cum.get(key, 0) + (o["step"] if o["needed"] else 0)
            o["cum"] = cum[key]
        self.totals = dict(cum)
        self.streams = {e: [] for e in self.CENG}
        self.queue_keys = {e: [] for e in self.DMAQ}
        for o in self.ops:
            self.streams[o["eng"]].append(o)
            if o["dma"] and o["skey"] not in self.queue_keys[o["eng"]]:
                self.queue_keys[o["eng"]].append(o["skey"])

    def run_stream(self, eng_obj, eng_name):
        sems = self.sems
        for e2 in self.CENG:
            if e2 != eng_name:
                sem, base = sems[e2]
                if base > 0:
                    eng_obj.wait_ge(sem, base)
        clock = {}
        for o in self.streams[eng_name]:
            for d in sorted(o["deps"]):
                pp = self.ops[d]
                if not SAFE_CHAIN and pp["eng"] == eng_name and not pp["dma"]:
                    continue
                if pp["dma"]:
                    key = pp["skey"]
                    sem, base = self.dsem[key]
                else:
                    key = pp["eng"]
                    sem, base = sems[key]
                need = pp["cum"]
                if clock.get(key, 0) >= need:
                    continue
                eng_obj.wait_ge(sem, base + need)
                clock[key] = need
            inst = o["fn"](eng_obj)
            if o["needed"]:
                if o["dma"]:
                    sem, base = self.dsem[o["skey"]]
                    inst.then_inc(sem, o["step"])
                else:
                    sem, base = sems[o["eng"]]
                    inst.then_inc(sem, o["step"])
        if eng_name in self.DMAQ:
            for key in self.queue_keys[eng_name]:
                sem, base = self.dsem[key]
                eng_obj.wait_ge(sem, base + self.totals.get(key, 0))
        sem, base = sems[eng_name]
        eng_obj.wait_ge(sem, base + self.totals.get(eng_name, 0))
        eng_obj.sem_inc(sem, 1)


def run_phase(nc, sems, dsem, build_fn, name):
    p = Prog(nc, dict(sems), dsem)
    build_fn(p)
    p.prepare()
    with nc.Block(name) as blk:
        blk.sync(lambda e: p.run_stream(e, "sp"))
        blk.tensor(lambda e: p.run_stream(e, "pe"))
        blk.vector(lambda e: p.run_stream(e, "dve"))
        blk.scalar(lambda e: p.run_stream(e, "act"))
        blk.gpsimd(lambda e: p.run_stream(e, "pool"))
    new = {}
    for k, (sem, base) in sems.items():
        new[k] = (sem, base + p.totals.get(k, 0) + 1)
    for key in dsem:
        if key != "__pool__":
            dsem[key][1] += p.totals.get(key, 0)
    return new


def ln_stats(p, stat, off, xsrc, sq_s, reads, n, epsc, negc, tag=""):
    sums = stat[:, off + 0:off + 1]
    sqs = stat[:, off + 1:off + 2]
    mean = stat[:, off + 2:off + 3]
    nv = stat[:, off + 3:off + 4]
    rstd = stat[:, off + 4:off + 5]
    mb = stat[:, off + 5:off + 6]
    st = ("st", off, tag)
    p.op("dve", lambda e: e.reduce_sum(sums, xsrc, axis=AX.X), reads=reads, writes=(st,))
    if "sqacc" in S1_NO:
        p.op("dve", lambda e: e.memset(sqs, 1000.0), reads=reads, writes=(st,))
    else:
        p.op("act", lambda e: e.activation(sq_s, xsrc, AF.Square, accum_out=sqs),
             reads=reads, writes=("sq_scr", st))
    p.op("dve", lambda e: e.tensor_scalar(mean, sums, 1.0 / n, None, OP.mult),
         reads=(st,), writes=(st,))
    p.op("dve", lambda e: e.tensor_scalar(nv, sqs, 1.0 / n, None, OP.mult),
         reads=(st,), writes=(st,))
    p.op("dve", lambda e: e.scalar_tensor_tensor(nv, mean, mean, nv, OP.mult,
                                                 OP.subtract),
         reads=(st,), writes=(st,))
    if "sqrt" in S1_NO:
        p.op("dve", lambda e: e.tensor_scalar(rstd, nv, 0.0, 1.0, OP.mult, OP.add),
             reads=(st, "cv"), writes=(st,))
    else:
        p.op("act", lambda e: e.activation(rstd, nv, AF.Sqrt, bias=epsc, scale=negc),
             reads=(st, "cv"), writes=(st,))
    p.op("dve", lambda e: e.reciprocal(rstd, rstd), reads=(st,), writes=(st,))
    p.op("dve", lambda e: e.tensor_scalar(mb, mean, rstd, -1.0, OP.mult, OP.mult),
         reads=(st,), writes=(st,))
    return rstd, mb, st


def build_nc():
    nc = bass.Bass("TRN2", target_bir_lowering=False,
                   detect_race_conditions=RACE_DET)
    W = {}

    def din(name, shape, dt=F32):
        W[name] = nc.dram_tensor(name, list(shape), dt, kind="ExternalInput").ap()

    def dram(name, shape, dt):
        W[name] = nc.dram_tensor(name, list(shape), dt, kind="Internal").ap()

    din("xs", (128 + SLAB, C))
    din("wkT", (C, C)); din("wrT", (C, C)); din("wvT", (C, C)); din("wgT", (C, C))
    din("woT", (C, C))
    din("alora", (C, 160))
    din("blora33", (5, 33, C))
    din("tdA", (C, 64))
    din("tdB65", (65, C))
    din("miux_col", (128, 8))
    din("u_col", (128, 8))
    din("gam_bc", (128, C)); din("bet_bc", (128, C))
    din("maskU2", (128, L))
    din("i128f", (128, 128)); din("i128b", (128, 128), BF16)
    din("selswap", (128, 128), BF16)
    din("hind8", (8, 128, 16), BF16)
    din("prevmask", (128, 8))
    din("cvec", (128, 4))
    din("wiT", (C, FF))
    din("woFT", (FF, C))
    din("wgateT", (C, E))
    din("bg_bc", (128, E))
    din("aAllT128", (C, 128))
    for _p in range(2):
        din(f"bT8P{_p}", (128, FF))
    out_ap = nc.dram_tensor("out", [SLAB, C], F32, kind="ExternalOutput").ap()

    dram("sp_y", (SLAB, C), BF16)
    dram("sp_r", (NT1, 8, 128, TT1), BF16)
    dram("sp_xg", (NT1, 8, 128, TT1), BF16)
    dram("sp_l2", (SLAB, C), BF16)
    dram("cc_in", (128, 520), F32)
    dram("cc_out", (1024, 520), F32)

    dbg_names = []
    def dbg_out(name, shape, dt):
        dbg_names.append(name)
        return nc.dram_tensor("dbg_" + name, list(shape), dt,
                              kind="ExternalOutput").ap()

    sems = {}
    for e in ("pe", "dve", "act", "pool", "sp"):
        sems[e] = (nc.alloc_semaphore(name="s_" + e), 0)
    dsem_pool = [nc.alloc_semaphore(name=f"d{i}") for i in range(96)]
    dsem = {"__pool__": dsem_pool}

    with nc.Block("init") as blk:
        def mk(engname):
            def f(eng):
                eng.sem_clear(sems[engname][0])
                if engname == "sp":
                    for s in dsem_pool:
                        eng.sem_clear(s)
                eng.sem_inc(sems[engname][0], 1)
                for e2 in ("pe", "dve", "act", "pool", "sp"):
                    eng.wait_ge(sems[e2][0], 1)
            return f
        blk.sync(mk("sp")); blk.tensor(mk("pe")); blk.vector(mk("dve"))
        blk.scalar(mk("act")); blk.gpsimd(mk("pool"))
    for e in ("pe", "dve", "act", "pool", "sp"):
        sems[e] = (sems[e][0], 1)

    with ExitStack() as big:
        sb = lambda n, sh, dt=F32: big.enter_context(nc.sbuf_tensor(n, list(sh), dt)).ap()
        S_f = sb("S_f", (128, 8, 64))
        S_b = sb("S_b", (128, 8, 64), BF16)
        s0_f = sb("s0_f", (128, 8, 64))
        dm_fm = sb("dm_fm", (128, 8, NCH))
        dmrun = sb("dmrun", (128, 8))
        miux_c = sb("miux_c", (128, 8))
        u_c = sb("u_c", (128, 8))
        maskU_c = sb("maskU_c", (128, L))
        i128f_c = sb("i128f_c", (128, 128))
        i128b_c = sb("i128b_c", (128, 128), BF16)
        selsw_c = sb("selsw_c", (128, 128), BF16)
        hind_c = sb("hind_c", (128, 8, 16), BF16)
        pmask_c = sb("pmask_c", (128, 8))
        gam_c = sb("gam_c", (128, C))
        bet_c = sb("bet_c", (128, C))
        zz64 = sb("zz64", (128, 64))
        cv = sb("cv", (128, 4))
        NEGC = cv[:, 0:1]
        EPS1 = cv[:, 1:2]
        EPSG = cv[:, 2:3]

        with ExitStack() as st1:
            sb1 = lambda n, sh, dt=F32: st1.enter_context(nc.sbuf_tensor(n, list(sh), dt)).ap()
            ps1 = lambda n, sh, dt=F32: st1.enter_context(nc.psum_tensor(n, list(sh), dt)).ap()
            wkT_b = sb1("wkT_b", (128, 8, C), BF16)
            wrT_b = sb1("wrT_b", (128, 8, C), BF16)
            wvT_b = sb1("wvT_b", (128, 8, C), BF16)
            alora_b = sb1("alora_b", (128, 8, 160), BF16)
            blora_b = [sb1(f"blora_b{f}", (33, C), BF16) for f in range(5)]
            tdA_b = sb1("tdA_b", (128, 8, 64), BF16)
            tdB_b = sb1("tdB_b", (65, C), BF16)
            x_tm = [sb1(f"x_tm{i}", (128, NB1, C)) for i in range(2)]
            xn_tm = sb1("xn_tm", (128, NB1, C))
            sq_s = sb1("sq_s", (128, C), BF16)
            stat = sb1("stat", (128, 16))
            xn_fm = sb1("xn_fm", (128, 8, TT1 + 8))
            xnb = sb1("xnb", (128, 8, TT1), BF16)
            halo = sb1("halo", (128, 8))
            dxa_b = sb1("dxa_b", (128, 8, TT1), BF16)
            xm_b = sb1("xm_b", (128, 8, TT1), BF16)
            xx_b = [sb1(f"xx_b{i}", (128, 8, TT1), BF16) for i in range(2)]
            lh_b = [sb1(f"lh_b{f}", (33, TT1), BF16) for f in range(5)]
            hw_b = sb1("hw_b", (65, TT1), BF16)
            k_b = sb1("k_b", (128, 8, TT1), BF16)
            r_b = sb1("r_b", (128, 8, TT1), BF16)
            kt_b = sb1("kt_b", (128, 8, TT1), BF16)
            rt_b = sb1("rt_b", (128, 8, TT1), BF16)
            khf_b = sb1("khf_b", (128, 8, TT1), BF16)
            khtm_b = sb1("khtm_b", (128, NB1, C), BF16)
            v_tm = sb1("v_tm", (128, NB1, C), BF16)
            v_sw = sb1("v_sw", (128, NB1, C), BF16)
            khsw = sb1("khsw", (128, NB1, C), BF16)
            ystg = sb1("ystg", (128, 512), BF16)
            ew_f = sb1("ew_f", (128, 8, TT1))
            d_f = ew_f
            ash = sb1("ash", (128, 8, CH1, L + 1))
            rca = sb1("rca", (128, 8, CH1, L + 1), BF16)
            urk = xm_b
            y_tm = sb1("y_tm", (128, NB1, C))
            ptm_b = sb1("ptm_b", (128, 16 * L), BF16)
            dtmp = sb1("dtmp", (128, C), BF16)
            c_tm = sb1("c_tm", (128, NB1 * 16))
            psAB = ps1("psAB", (128, 1024))
            psA = psAB[:, 0:512]
            psB = psAB[:, 512:1024]
            psS = ps1("psS", (128, 512))
            psY = [ps1("psY1", (128, 1024)), ps1("psY2", (128, 1024))]
            psT = ps1("psT", (128, 512))

            def ln1(p, xsrc, xres, dst, off):
                rstd, mb, st = ln_stats(p, stat, off, xsrc, sq_s[:], xres, C, EPS1, NEGC)
                p.op("act", lambda e: e.activation(dst, xsrc, AF.Identity, bias=mb,
                                                   scale=rstd),
                     reads=xres + (st,), writes=("xn_tm",))

            def transpose8(p, src_tm, srcres, out_fn):
                def tp(e):
                    last = None
                    ncc = 4 if "tp4" in S1_NO else 8
                    for cc in range(ncc):
                        last = e.matmul(psAB[:, 128 * cc:128 * (cc + 1)],
                                        src_tm[:, 128 * cc:128 * (cc + 1)],
                                        i128f_c, start=True, stop=True)
                    return last
                p.op("pe", tp, reads=srcres + ("i128f",),
                     writes=("psA", "psB"), tag="tp")
                out_fn(p)

            def proj_fm(p, xb, xbres, wT, wres, dst, dres):
                for jp in range(4):
                    pst = (psA, psB)[jp % 2]
                    pres = ("psA", "psB")[jp % 2]
                    def kmm(e, jp=jp, pst=pst):
                        last = None
                        for jj in range(2):
                            j = 2 * jp + jj
                            for cc in range(8):
                                last = e.matmul(pst[:, 256 * jj:256 * (jj + 1)],
                                                wT[:, cc, 128 * j:128 * (j + 1)],
                                                xb[:, cc, :],
                                                start=(cc == 0), stop=(cc == 7))
                        return last
                    p.op("pe", kmm, reads=(xbres, wres), writes=(pres,))
                    p.op("act", lambda e, jp=jp, pst=pst: e.activation(
                        dst[:, 2 * jp:2 * jp + 2, :],
                        pst.rearrange("p (j n) -> p j n", j=2), AF.Copy),
                        reads=(pres,), writes=(dres,))

            def proj_tm(p, xb, xbres, wT, wres, dst, dres, func=AF.Copy):
                for bi in range(NB1):
                    pyp = psY[bi]
                    pyres = ("psY1", "psY2")[bi]
                    def vmm(e, bi=bi, pyp=pyp):
                        last = None
                        for cc in range(8):
                            for nn in range(2):
                                last = e.matmul(pyp[:, 512 * nn:512 * (nn + 1)],
                                                xb[:, cc, 128 * bi:128 * (bi + 1)],
                                                wT[:, cc, 512 * nn:512 * (nn + 1)],
                                                start=(cc == 0), stop=(cc == 7))
                        return last
                    p.op("pe", vmm, reads=(xbres, wres), writes=(pyres,))
                    p.op("act", lambda e, bi=bi, pyp=pyp: e.activation(
                        dst[:, bi, :], pyp[:], func),
                        reads=(pyres,), writes=(dres,))

            def front1(p, t, bufi):
                xres = (("x", bufi),)
                if STAGE_LIM < 2:
                    return
                for bi in range(NB1):
                    ln1(p, x_tm[bufi][:, bi, :], xres, xn_tm[:, bi, :], 8 * bi)
                if STAGE_LIM < 2.2:
                    return
                p.op("dve", lambda e: e.tensor_copy(xn_fm[:, :, 7:8],
                                                    halo[:, :].unsqueeze(2)),
                     reads=("halo",), writes=("xnfm",))
                if STAGE_LIM < 2.4:
                    return
                for bi in range(NB1):
                    for q4 in range(2):
                        def ftp(e, bi=bi, q4=q4):
                            last = None
                            for cc in range(4 * q4, 4 * q4 + 4):
                                last = e.matmul(
                                    psT[:, 128 * (cc % 4):128 * (cc % 4 + 1)],
                                    xn_tm[:, bi, 128 * cc:128 * (cc + 1)],
                                    i128f_c, start=True, stop=True)
                            return last
                        p.op("pe", ftp, reads=("xn_tm", "i128f"),
                             writes=("psT",), tag="tp")
                        p.op("dve", lambda e, bi=bi, q4=q4: e.tensor_copy(
                            xn_fm[:, 4 * q4:4 * q4 + 4,
                                  8 + 128 * bi:136 + 128 * bi],
                            psT.rearrange("p (j n) -> p j n", j=4)),
                            reads=("psT",), writes=("xnfm",))
                if STAGE_LIM < 2.6:
                    return
                p.op("dve", lambda e: e.tensor_copy(halo[:, :],
                                                    xn_fm[:, :, TT1 + 7:TT1 + 8].squeeze(2)),
                     reads=("xnfm",), writes=("halo",))
                p.op("dve", lambda e: e.tensor_copy(xnb[:], xn_fm[:, :, 8:TT1 + 8]),
                     reads=("xnfm",), writes=("xnb",))
                p.op("dve", lambda e: e.tensor_sub(dxa_b[:], xn_fm[:, :, 7:TT1 + 7],
                                                   xn_fm[:, :, 8:TT1 + 8]),
                     reads=("xnfm",), writes=("dxa",))
                if STAGE_LIM < 3:
                    return
                for cc in range(8):
                    p.op("dve", lambda e, cc=cc: e.scalar_tensor_tensor(
                        xm_b[:, cc, :], dxa_b[:, cc, :], miux_c[:, cc:cc + 1],
                        xnb[:, cc, :], OP.mult, OP.add),
                        reads=("dxa", "miux", "xnb"), writes=("xm", "urk"))

            def tile1(p, t, bufi, mid=None):
                if STAGE_LIM < 4:
                    return

                def lmm(e):
                    last = None
                    for f in range(5):
                        pst = (psAB[0:32, 0:256], psAB[0:32, 256:512],
                               psAB[0:32, 512:768], psAB[0:32, 768:1024],
                               psS[0:32, 0:256])[f]
                        for cc in range(8):
                            last = e.matmul(pst, alora_b[:, cc, 32 * f:32 * f + 32],
                                            xm_b[:, cc, :], start=(cc == 0), stop=(cc == 7))
                    return last
                p.op("pe", lmm, reads=("xm", "al"), writes=("psA", "psB", "psS"))
                for f in range(5):
                    pst = (psAB[0:32, 0:256], psAB[0:32, 256:512],
                           psAB[0:32, 512:768], psAB[0:32, 768:1024],
                           psS[0:32, 0:256])[f]
                    pres = ("psA", "psA", "psB", "psB", "psS")[f]
                    p.op("act", lambda e, f=f, pst=pst: e.activation(
                        lh_b[f][0:32, :], pst, AF.Tanh),
                        reads=(pres,), writes=(("lh", f),))

                if STAGE_LIM < 5:
                    return
                pending = [None]
                for f in range(5):
                    if STAGE_LIM < 5 + f:
                        break
                    xb = xx_b[f % 2]
                    xbres = ("xx", f % 2)
                    for half in range(2):
                        pyp = psY[half]
                        pyres = ("psY1", "psY2")[half]
                        def mmm(e, f=f, half=half, pyp=pyp):
                            last = None
                            for j in range(4):
                                cj = 4 * half + j
                                last = e.matmul(pyp[:, 256 * j:256 * (j + 1)],
                                                blora_b[f][0:33, 128 * cj:128 * (cj + 1)],
                                                lh_b[f][0:33, :], start=True, stop=True)
                            return last
                        p.op("pe", mmm, reads=(("lh", f), f"bl{f}"), writes=(pyres,))

                        p.op("dve", lambda e, half=half, pyp=pyp, xb=xb: e.tensor_mul(
                            xb[:, 4 * half:4 * half + 4, :],
                            dxa_b[:, 4 * half:4 * half + 4, :],
                            pyp.rearrange("p (j n) -> p j n", j=4)),
                            reads=("dxa", ("psY1", "psY2")[half]), writes=(xbres,))
                    p.op("dve", lambda e, xb=xb: e.tensor_add(xb[:], xb[:], xnb[:]),
                         reads=(xbres, "xnb"), writes=(xbres,))
                    if pending[0] is not None:
                        pending[0]()
                        pending[0] = None

                    def wmm2_rnd(rnd):
                        for half in range(2):
                            pst = (psA, psB)[half]
                            pres = ("psA", "psB")[half]
                            def wmm2(e, rnd=rnd, half=half, pst=pst):
                                last = None
                                for j in range(2):
                                    cj = 4 * rnd + 2 * half + j
                                    last = e.matmul(pst[:, 256 * j:256 * (j + 1)],
                                                    tdB_b[0:65, 128 * cj:128 * (cj + 1)],
                                                    hw_b[0:65, :], start=True, stop=True)
                                return last
                            p.op("pe", wmm2, reads=("hw", "tdB"), writes=(pres,))
                            p.op("act", lambda e, rnd=rnd, half=half, pst=pst:
                                 e.activation(
                                     ew_f[:, 4 * rnd + 2 * half:4 * rnd + 2 * half + 2, :],
                                     pst.rearrange("p (j n) -> p j n", j=2), AF.Exp),
                                 reads=(pres,), writes=("ew",))

                    def extras(f=f, xb=xb, xbres=xbres):
                      if f == 0:
                          def wmm1(e, xb=xb):
                              last = None
                              for cc in range(8):
                                  last = e.matmul(psS[0:64, 0:TT1], tdA_b[:, cc, :],
                                                  xb[:, cc, :], start=(cc == 0), stop=(cc == 7))
                              return last
                          p.op("pe", wmm1, reads=(xbres, "tdA"), writes=("psS",))
                          p.op("act", lambda e: e.activation(hw_b[0:64, :],
                                                             psS[0:64, 0:TT1], AF.Tanh),
                               reads=("psS",), writes=("hw",))
                      elif f == 1:
                          wmm2_rnd(0)
                          proj_fm(p, xb, xbres, wkT_b, "wk", k_b, "k")
                      elif f == 2:
                          wmm2_rnd(1)
                          proj_tm(p, xb, xbres, wvT_b, "wv", v_tm, "v")
                      elif f == 3:
                          p.op("act", lambda e: e.activation(d_f[:], ew_f[:], AF.Exp,
                                                             scale=-1.0),
                               reads=("ew",), writes=("ew", "d"))
                          def scn(e):
                              last = None
                              for cc in range(8):
                                  for m in range(CH1):
                                      last = e.tensor_tensor_scan(
                                          ash[:, cc, m, 1:L + 1],
                                          d_f[:, cc, L * m:L * (m + 1)],
                                          zz64[:, 0:L], 1.0, OP.mult, OP.add)
                              return last
                          p.op("dve", scn, reads=("d", "zz"), writes=("ash",))
                          def rcp(e):
                              with nc.allow_low_precision(reason="1/A as bf16 matmul operand"):
                                  return e.reciprocal(
                                      rca.rearrange("p c m l -> p (c m l)"),
                                      ash.rearrange("p c m l -> p (c m l)"))
                          p.op("dve", rcp, reads=("ash",), writes=("rca",))
                          proj_fm(p, xb, xbres, wrT_b, "wr", r_b, "r")
                      elif f == 4:
                          p.dma("pool", W["sp_xg"][t].rearrange("c p n -> p c n"), xb[:],
                                reads=(xbres,), writes=(), tag="spxg")
                    pending[0] = extras


                if pending[0] is not None:
                    pending[0]()
                    pending[0] = None

                if STAGE_LIM < 11:
                    return
                kv = lambda z: z.rearrange("p c (m l) -> p c m l", l=L)
                p.op("dve", lambda e: e.tensor_mul(kv(kt_b), kv(k_b),
                                                   rca[:, :, :, 1:L + 1]),
                     reads=("k", "rca"), writes=("kt",))
                p.op("dve", lambda e: e.tensor_mul(
                    kv(khf_b), kv(kt_b),
                    ash[:, :, :, L:L + 1].broadcast_to([128, 8, CH1, L])),
                    reads=("kt", "ash"), writes=("khf",))
                p.op("dve", lambda e: e.tensor_mul(kv(rt_b), kv(r_b), ash[:, :, :, 0:L]),
                     reads=("r", "ash"), writes=("rt",))
                p.op("dve", lambda e: e.tensor_mul(urk[:], r_b[:], k_b[:]),
                     reads=("r", "k"), writes=("urk", "xm"))
                p.op("dve", lambda e: e.tensor_mul(
                    urk[:], urk[:],
                    u_c[:, :].unsqueeze(2).broadcast_to([128, 8, TT1])),
                    reads=("urk", "u"), writes=("urk",))

                def cmm(e):
                    last = None
                    for bi in range(NB1):
                        for cc in range(8):
                            last = e.matmul(psS[:, 16 * bi:16 * bi + 16],
                                            urk[:, cc, 128 * bi:128 * (bi + 1)],
                                            hind_c[:, cc, :], start=(cc == 0), stop=(cc == 7))
                    return last
                p.op("pe", cmm, reads=("urk", "hind"), writes=("psS",))
                p.op("dve", lambda e: e.tensor_copy(c_tm[:], psS[:, 0:16 * NB1]),
                     reads=("psS",), writes=("c",))

                for bi in range(NB1):
                    for q4 in range(2):
                        def ktp(e, bi=bi, q4=q4):
                            last = None
                            for cc in range(4 * q4, 4 * q4 + 4):
                                last = e.matmul(psT[:, 128 * (cc % 4):128 * (cc % 4 + 1)],
                                                khf_b[:, cc, 128 * bi:128 * (bi + 1)],
                                                i128b_c, start=True, stop=True)
                            return last
                        p.op("pe", ktp, reads=("khf", "i128b"), writes=("psT",))
                        p.op("dve", lambda e, bi=bi, q4=q4: e.tensor_copy(
                            khtm_b[:, bi, 512 * q4:512 * (q4 + 1)], psT[:]),
                            reads=("psT",), writes=("khtm",))
                    for nn in range(2):
                        p.op("pe", lambda e, bi=bi, nn=nn: e.matmul(
                            psT[:], selsw_c, khtm_b[:, bi, 512 * nn:512 * (nn + 1)],
                            start=True, stop=True),
                            reads=("khtm", "selsw"), writes=("psT",), tag="khsw")
                        p.op("dve", lambda e, bi=bi, nn=nn: e.tensor_copy(
                            khsw[:, bi, 512 * nn:512 * (nn + 1)], psT[:]),
                            reads=("psT",), writes=(("khsw", bi),), tag="khswv")
                        p.op("pe", lambda e, bi=bi, nn=nn: e.matmul(
                            psT[:], selsw_c, v_tm[:, bi, 512 * nn:512 * (nn + 1)],
                            start=True, stop=True),
                            reads=("v", "selsw"), writes=("psT",), tag="vsw")
                        p.op("dve", lambda e, bi=bi, nn=nn: e.tensor_copy(
                            v_sw[:, bi, 512 * nn:512 * (nn + 1)], psT[:]),
                            reads=("psT",), writes=(("vsw", bi),), tag="vswv")

                if STAGE_LIM < 12:
                    return
                for m in range(CH1):
                    bm = m // 2
                    mp = m % 2
                    pyp = psY[bm]
                    pyo = 512 * mp
                    pyres = ("psY1", "psY2")[bm]

                    def ptmm(e, m=m):
                        last = None
                        for h in range(H):
                            ch, hh = h // 2, (h % 2) * 64
                            last = e.matmul(psAB[hh:hh + 64, 64 * ch:64 * ch + 64],
                                            kt_b[hh:hh + 64, ch, L * m:L * (m + 1)],
                                            rt_b[hh:hh + 64, ch, L * m:L * (m + 1)],
                                            start=True, stop=True)
                        return last
                    p.op("pe", ptmm, reads=("kt", "rt"), writes=("psA", "psB"))
                    pb = ptm_b
                    p.op("dve", lambda e: e.tensor_mul(
                        ptm_b[:, 0:512].rearrange("p (c l) -> p c l", c=8),
                        psAB[:, 0:512].rearrange("p (c l) -> p c l", c=8),
                        maskU_c[:, :].unsqueeze(1).broadcast_to([128, 8, L])),
                        reads=("psA", "psB", "maskU"), writes=(("ptm", m % 2),))

                    def ymm(e, m=m, bm=bm, mp=mp, pyp=pyp, pyo=pyo):
                        last = None
                        for h in range(H):
                            ch, hh = h // 2, (h % 2) * 64
                            if hh == 64 * mp:
                                vsl = v_tm[hh:hh + 64, bm, 64 * h:64 * h + 64]
                            else:
                                vsl = v_sw[hh:hh + 64, bm, 64 * h:64 * h + 64]
                            last = e.matmul(pyp[hh:hh + 64, pyo + 64 * ch:pyo + 64 * ch + 64],
                                            ptm_b[hh:hh + 64, 64 * ch:64 * ch + 64],
                                            vsl, start=True, stop=False)
                            last = e.matmul(pyp[hh:hh + 64, pyo + 64 * ch:pyo + 64 * ch + 64],
                                            rt_b[hh:hh + 64, ch, L * m:L * (m + 1)],
                                            S_b[hh:hh + 64, ch, :], start=False, stop=True)
                        return last
                    p.op("pe", ymm, reads=(("ptm", m % 2), "v", ("vsw", bm), "rt", "Sb"),
                         writes=(pyres,))

                    def smm(e, m=m, bm=bm, mp=mp):
                        last = None
                        for h in range(H):
                            ch, hh = h // 2, (h % 2) * 64
                            if hh == 64 * mp:
                                ksl = khtm_b[hh:hh + 64, bm, 64 * h:64 * h + 64]
                                vsl = v_tm[hh:hh + 64, bm, 64 * h:64 * h + 64]
                            else:
                                ksl = khsw[hh:hh + 64, bm, 64 * h:64 * h + 64]
                                vsl = v_sw[hh:hh + 64, bm, 64 * h:64 * h + 64]
                            last = e.matmul(psS[hh:hh + 64, 64 * ch:64 * ch + 64],
                                            ksl, vsl, start=True, stop=True)
                        return last
                    p.op("pe", smm, reads=("khtm", ("khsw", bm), "v", ("vsw", bm)),
                         writes=("psS",))
                    p.op("dve", lambda e, m=m: e.tensor_mul(
                        S_f[:], S_f[:],
                        ash[:, :, m, L:L + 1].broadcast_to([128, 8, 64])),
                        reads=("S", "ash"), writes=("S",))
                    p.op("dve", lambda e: e.tensor_add(
                        S_f[:], S_f[:],
                        psS[:, 0:512].rearrange("p (c n) -> p c n", c=8)),
                        reads=("S", "psS"), writes=("S",))
                    p.op("act", lambda e: e.activation(
                        S_b.rearrange("p c n -> p (c n)"),
                        S_f.rearrange("p c n -> p (c n)"), AF.Copy),
                        reads=("S",), writes=("Sb",))
                    gm = CH1 * t + m
                    p.op("dve", lambda e, gm=gm: e.tensor_copy(
                        dm_fm[:, :, gm:gm + 1], dmrun[:, :].unsqueeze(2)),
                        reads=("dmrun",), writes=("dm",))
                    p.op("dve", lambda e, m=m: e.tensor_mul(dmrun[:], dmrun[:],
                                                            ash[:, :, m, L]),
                         reads=("dmrun", "ash"), writes=("dmrun",))

                    mo = 64 * mp
                    def dg(e, bm=bm, mo=mo, m=m):
                        return e.tensor_mul(
                            dtmp[mo:mo + 64, :].rearrange("p (h n) -> p h n", h=H),
                            v_tm[mo:mo + 64, bm, :].rearrange("p (h n) -> p h n", h=H),
                            c_tm[mo:mo + 64, 16 * bm:16 * bm + 16].unsqueeze(2)
                                .broadcast_to([64, H, HS]))
                    p.op("dve", dg, reads=("v", "c"), writes=("dtmp",))
                    def yev1(e, pyp=pyp, pyo=pyo, mo=mo, bm=bm, mp=mp):
                        yv = y_tm[mo:mo + 64, bm, :].rearrange(
                            "p (c two l) -> p c two l", two=2, l=HS)[:, :, mp, :]
                        dv = dtmp[mo:mo + 64, :].rearrange(
                            "p (c two l) -> p c two l", two=2, l=HS)[:, :, mp, :]
                        return e.tensor_add(
                            yv,
                            pyp[mo:mo + 64, pyo:pyo + 512]
                            .rearrange("p (c l) -> p c l", l=HS),
                            dv)
                    p.op("dve", yev1, reads=(pyres, "dtmp"), writes=("y",))
                    def yst(e, pyp=pyp, pyo=pyo, mp=mp):
                        so = 64 * (1 - mp)
                        return e.tensor_copy(ystg[so:so + 64, :],
                                             pyp[so:so + 64, pyo:pyo + 512])
                    p.op("dve", yst, reads=(pyres,), writes=("ystg",))
                    def ysw(e):
                        with nc.allow_low_precision(reason="bf16 y swap"):
                            return e.matmul(psS[:, 0:512], selsw_c, ystg[:],
                                            start=True, stop=True)
                    p.op("pe", ysw, reads=("ystg", "selsw"), writes=("psS",), tag="ysw")
                    def yev2(e, mo=mo, bm=bm, mp=mp):
                        yv = y_tm[mo:mo + 64, bm, :].rearrange(
                            "p (c two l) -> p c two l", two=2, l=HS)[:, :, 1 - mp, :]
                        dv = dtmp[mo:mo + 64, :].rearrange(
                            "p (c two l) -> p c two l", two=2, l=HS)[:, :, 1 - mp, :]
                        return e.tensor_add(
                            yv,
                            psS[mo:mo + 64, 0:512]
                            .rearrange("p (c l) -> p c l", l=HS),
                            dv)
                    p.op("dve", yev2, reads=("psS", "dtmp"), writes=("y",))
                    if m == 1 and mid is not None:
                        mid()

                p.dma("pool", W["sp_y"][t * TT1:(t + 1) * TT1, :]
                      .rearrange("(b q) f -> q b f", q=128), y_tm[:],
                      reads=("y",), writes=(), tag="spy")
                p.dma("act", W["sp_r"][t].rearrange("c p n -> p c n"), rt_b[:],
                      reads=("rt",), writes=(), tag="spr")

                if t == DBG_TILE:
                    dbg1 = {"xn_fm": (xn_fm, "xnfm"), "xnb": (xnb, "xnb"),
                            "dxa": (dxa_b, "dxa"), "xm": (xm_b, "urk"),
                            "k": (k_b, "k"), "r": (r_b, "r"), "v": (v_tm, "v"),
                            "ew": (ew_f, "ew"), "d": (d_f, "d"), "ash": (ash, "ash"),
                            "kt": (kt_b, "kt"), "rt": (rt_b, "rt"),
                            "khf": (khf_b, "khf"), "khtm": (khtm_b, "khtm"),
                            "urk": (urk, "urk"), "c": (c_tm, "c"), "y": (y_tm, "y"),
                            "S": (S_f, "S"), "xx0": (xx_b[0], ("xx", 0)),
                            "xx1": (xx_b[1], ("xx", 1)), "dm": (dm_fm, "dm")}
                    for nm, (src, res) in dbg1.items():
                        da = dbg_out(nm, list(src.shape), src.dtype)
                        p.dma("sp", da, src, reads=(res,), tag="dbg")

            def phase1(p):
                p.dma("sp", miux_c, W["miux_col"], writes=("miux",))
                p.dma("sp", u_c, W["u_col"], writes=("u",))
                p.dma("sp", maskU_c, W["maskU2"], writes=("maskU",))
                p.dma("sp", i128f_c, W["i128f"], writes=("i128f",))
                p.dma("sp", i128b_c, W["i128b"], writes=("i128b",))
                p.dma("sp", selsw_c, W["selswap"], writes=("selsw",))
                p.dma("sp", hind_c, W["hind8"].rearrange("c p h -> p c h"),
                      writes=("hind",))
                p.dma("sp", pmask_c, W["prevmask"], writes=("pmask",))
                p.dma("sp", cv, W["cvec"], writes=("cv",))
                p.dma("pool", wkT_b, W["wkT"].rearrange("(c p) n -> p c n", p=128),
                      writes=("wk",))
                p.dma("pool", wrT_b, W["wrT"].rearrange("(c p) n -> p c n", p=128),
                      writes=("wr",))
                p.dma("pool", wvT_b, W["wvT"].rearrange("(c p) n -> p c n", p=128),
                      writes=("wv",))
                p.dma("pool", alora_b, W["alora"].rearrange("(c p) n -> p c n", p=128),
                      writes=("al",))
                for f in range(5):
                    p.dma("pool", blora_b[f], W["blora33"][f], writes=(f"bl{f}",))
                p.dma("pool", tdA_b, W["tdA"].rearrange("(c p) n -> p c n", p=128),
                      writes=("tdA",))
                p.dma("pool", tdB_b, W["tdB65"], writes=("tdB",))

                p.op("dve", lambda e: e.memset(S_f[:], 0.0), writes=("S",))
                p.op("dve", lambda e: e.memset(xn_fm[:, :, 0:8], 0.0), writes=("xnfm",))
                p.op("dve", lambda e: e.memset(dm_fm[:], 0.0), writes=("dm",))
                p.op("dve", lambda e: e.memset(ew_f[:], 0.0), writes=("ew",))
                p.op("dve", lambda e: e.memset(ash[:], 0.0), writes=("ash",))
                p.op("dve", lambda e: e.memset(ystg[:], 0.0), writes=("ystg",))
                p.op("dve", lambda e: e.memset(S_b[:], 0.0), writes=("Sb",))
                p.op("dve", lambda e: e.memset(dmrun[:], 1.0), writes=("dmrun",))
                p.op("dve", lambda e: e.memset(halo[:], 0.0), writes=("halo",))
                p.op("dve", lambda e: e.memset(ash[:, :, :, 0:1], 1.0), writes=("ash",))
                p.op("dve", lambda e: e.memset(zz64[:], 0.0), writes=("zz",))
                for f in range(5):
                    p.op("dve", lambda e, f=f: e.memset(lh_b[f][32:33, :], 1.0),
                         writes=(("lh", f),))
                p.op("dve", lambda e: e.memset(hw_b[64:65, :], 1.0), writes=("hw",))

                if STAGE_LIM < 1:
                    return
                p.dma("sp", x_tm[1][:, 0, :], W["xs"][0:128, :], writes=(("x", 1),))
                ln1(p, x_tm[1][:, 0, :], (("x", 1),), xn_tm[:, 0, :], 0)
                def hevac(p_):
                    if "hv2" in S1_NO:
                        p_.op("dve", lambda e: e.tensor_copy(
                            halo[:, 0:4],
                            psAB.rearrange("p (j n) -> p j n", j=8)[:, 0:4, 127]),
                            reads=("psA", "psB"), writes=("halo",))
                        p_.op("dve", lambda e: e.tensor_copy(
                            halo[:, 4:8],
                            psAB.rearrange("p (j n) -> p j n", j=8)[:, 4:8, 127]),
                            reads=("psA", "psB"), writes=("halo",))
                    else:
                        p_.op("dve", lambda e: e.tensor_copy(
                            halo[:, :],
                            psAB.rearrange("p (j n) -> p j n", j=8)[:, :, 127]),
                            reads=("psA", "psB"), writes=("halo",))
                transpose8(p, xn_tm[:, 0, :], ("xn_tm",), hevac)

                NT1e = TILE_LIM if TILE_LIM is not None else NT1

                def ldx(t):
                    bufi = t % 2
                    rows0 = 128 + t * TT1
                    p.dma("sp", x_tm[bufi],
                          W["xs"][rows0:rows0 + TT1, :]
                          .rearrange("(b q) f -> q b f", q=128),
                          writes=(("x", bufi),), tag=f"ldx{t}")

                ldx(0)
                front1(p, 0, 0)
                for t in range(NT1e):
                    if t + 1 < NT1e:
                        ldx(t + 1)

                    def mid(tt=t):
                        if tt + 1 < NT1e:
                            front1(p, tt + 1, (tt + 1) % 2)
                    tile1(p, t, t % 2, mid)

                p.dma("pool", W["cc_in"][:, 0:512],
                      S_f.rearrange("p c n -> p (c n)"), reads=("S",), writes=("ccin",))
                p.dma("pool", W["cc_in"][:, 512:520], dmrun[:, :],
                      reads=("dmrun",), writes=("ccin",))

            if "1" in PHASES:
                sems = run_phase(nc, sems, dsem, phase1, "p1")

        with ExitStack() as st2:
            sb2 = lambda n, sh, dt=F32: st2.enter_context(nc.sbuf_tensor(n, list(sh), dt)).ap()
            gq_b = sb2("gq_b", (128, 520))
            coef = sb2("coef", (128, 8))
            fcol = sb2("fcol", (128, 8))
            t8 = sb2("t8", (128, 8))
            wgT_b3 = sb2("wgT_b3", (128, 8, C), BF16)
            woT_b3 = sb2("woT_b3", (128, 8, C), BF16)

            def phase2(p):
                p.op("pool", lambda e: e.collective_compute(
                    "AllGather", OP.bypass,
                    replica_groups=[[0, 1, 2, 3, 4, 5, 6, 7]],
                    ins=[W["cc_in"].opt()], outs=[W["cc_out"].opt()]),
                    reads=(), writes=("ccout",), dma=True, tag="ag", step=1)
                p.dma("pool", wgT_b3, W["wgT"].rearrange("(c p) n -> p c n", p=128),
                      writes=("wg",))
                p.dma("pool", woT_b3, W["woT"].rearrange("(c p) n -> p c n", p=128),
                      writes=("wo",))
                p.dma("sp", gam_c, W["gam_bc"], writes=("gam",))
                p.dma("sp", bet_c, W["bet_bc"], writes=("bet",))
                p.op("dve", lambda e: e.memset(s0_f[:], 0.0), writes=("s0",))
                p.op("dve", lambda e: e.memset(fcol[:], 1.0), writes=("f",))
                for q in range(7, -1, -1):
                    p.dma("sp", gq_b, W["cc_out"][128 * q:128 * (q + 1), :],
                          reads=("ccout",), writes=("gq",), tag=f"gq{q}")
                    p.op("dve", lambda e, q=q: e.tensor_scalar(
                        coef[:], fcol[:], pmask_c[:, q:q + 1], None, OP.mult),
                        reads=("f", "gq"), writes=("coef",))
                    for pp in range(8):
                        p.op("dve", lambda e, pp=pp: e.scalar_tensor_tensor(
                            s0_f[:, pp, :], gq_b[:, 64 * pp:64 * pp + 64],
                            coef[:, pp:pp + 1], s0_f[:, pp, :], OP.mult, OP.add),
                            reads=("gq", "coef", "s0"), writes=("s0",))
                    p.op("dve", lambda e, q=q: e.tensor_scalar(
                        t8[:], gq_b[:, 512:520], -1.0, pmask_c[:, q:q + 1],
                        OP.add, OP.mult), reads=("gq",), writes=("t8",))
                    p.op("dve", lambda e: e.tensor_scalar(t8[:], t8[:], 1.0, None,
                                                          OP.add),
                         reads=("t8",), writes=("t8",))
                    p.op("dve", lambda e: e.tensor_mul(fcol[:], fcol[:], t8[:]),
                         reads=("f", "t8", "gq"), writes=("f",))

            if "2" in PHASES:
                sems = run_phase(nc, sems, dsem, phase2, "p2")

            with ExitStack() as st3:
                sb3 = lambda n, sh, dt=F32: st3.enter_context(nc.sbuf_tensor(n, list(sh), dt)).ap()
                ps3 = lambda n, sh, dt=F32: st3.enter_context(nc.psum_tensor(n, list(sh), dt)).ap()
                yl_b2 = [sb3(f"yl_b{i}", (128, NB3A, C), BF16) for i in range(2)]
                rt3_b2 = [sb3(f"rt3_b{i}", (128, 8, TT3A), BF16) for i in range(2)]
                xg3_b2 = [sb3(f"xg3_b{i}", (128, 8, TT3A), BF16) for i in range(2)]
                g3_tm = sb3("g3_tm", (128, NB3A, C), BF16)
                s0m_b = sb3("s0m_b", (128, 8, 64), BF16)
                yfull = sb3("yfull", (128, NB3A, C))
                sq3_s = sb3("sq3_s", (128, C), BF16)
                st3a = sb3("st3a", (128, 192))
                zt3 = sb3("zt3", (128, C))
                gp3 = sb3("gp3", (128, C), BF16)
                bp3 = sb3("bp3", (128, C), BF16)
                z3_b = sb3("z3_b", (128, NB3A, C), BF16)
                zf3_b = sb3("zf3_b", (128, 8, TT3A), BF16)
                ao3 = sb3("ao3", (128, C))
                l2_b = sb3("l2_b", (128, NB3A, C), BF16)
                psY3 = [ps3("psY31", (128, 1024)), ps3("psY32", (128, 1024))]
                psT3 = ps3("psT3", (128, 1024))
                psS3 = ps3("psS3", (128, 512))
                ystg3 = sb3("ystg3", (128, 512), BF16)

                def phase3a(p):
                    for t in range(NT3A):
                        r0 = t * TT3A
                        tp_ = t % 2
                        yl_b = yl_b2[tp_]
                        rt3_b = rt3_b2[tp_]
                        xg3_b = xg3_b2[tp_]
                        ylr = ("yl", tp_)
                        rtr = ("rt3", tp_)
                        xgr = ("xg3", tp_)
                        p.dma("sp", yl_b,
                              W["sp_y"][r0:r0 + TT3A, :]
                              .rearrange("(b q) f -> q b f", q=128),
                              writes=(ylr,), tag=f"ldy{t}")
                        for i in range(2):
                            tt = 2 * t + i
                            p.dma("sp", rt3_b[:, :, 256 * i:256 * (i + 1)],
                                  W["sp_r"][tt].rearrange("c p n -> p c n"),
                                  writes=(rtr,))
                            p.dma("act", xg3_b[:, :, 256 * i:256 * (i + 1)],
                                  W["sp_xg"][tt].rearrange("c p n -> p c n"),
                                  writes=(xgr,))
                        for bi in range(NB3A):
                            pyp = psY3[bi % 2]
                            pyres = ("psY31", "psY32")[bi % 2]
                            def gmm(e, bi=bi, pyp=pyp, xg3_b=xg3_b):
                                last = None
                                for cc in range(8):
                                    for nn in range(2):
                                        last = e.matmul(
                                            pyp[:, 512 * nn:512 * (nn + 1)],
                                            xg3_b[:, cc, 128 * bi:128 * (bi + 1)],
                                            wgT_b3[:, cc, 512 * nn:512 * (nn + 1)],
                                            start=(cc == 0), stop=(cc == 7))
                                return last
                            p.op("pe", gmm, reads=(xgr, "wg"), writes=(pyres,))
                            p.op("act", lambda e, bi=bi, pyp=pyp: e.activation(
                                g3_tm[:, bi, :], pyp[:], AF.Silu),
                                reads=(pyres,), writes=("g3",))
                        if t == 0:
                            p.op("dve", lambda e: e.memset(ystg3[:], 0.0),
                                 writes=("ystg3",))
                        def gn_block(bi):
                            off = 16 * bi
                            ys = yfull[:, bi, :]
                            su = st3a[:, off:off + 16]
                            sq = st3a[:, 64 + off:64 + off + 16]
                            yv = ys.rearrange("p (h n) -> p h n", h=H)
                            p.op("dve", lambda e, yv=yv, su=su: e.reduce_sum(
                                su.unsqueeze(2), yv, axis=AX.X),
                                reads=("yf",), writes=("st3",))
                            p.op("act", lambda e, ys=ys: e.activation(
                                sq3_s[:], ys, AF.Square),
                                reads=("yf",), writes=("sq3",))
                            p.op("dve", lambda e, sq=sq: e.reduce_sum(
                                sq.unsqueeze(2),
                                sq3_s.rearrange("p (h n) -> p h n", h=H), axis=AX.X),
                                reads=("sq3",), writes=("st3",))
                            p.op("dve", lambda e, su=su: e.tensor_scalar(
                                su, su, 1.0 / HS, None, OP.mult),
                                reads=("st3",), writes=("st3",))
                            p.op("dve", lambda e, su=su, sq=sq: e.tensor_scalar(
                                sq, sq, 1.0 / HS, None, OP.mult),
                                reads=("st3",), writes=("st3",))
                            m2 = st3a[:, 128 + off:128 + off + 16]
                            p.op("dve", lambda e, su=su, m2=m2: e.tensor_mul(m2, su, su),
                                 reads=("st3",), writes=("st3",))
                            p.op("dve", lambda e, sq=sq, m2=m2: e.tensor_sub(sq, m2, sq),
                                 reads=("st3",), writes=("st3",))
                            p.op("act", lambda e, sq=sq: e.activation(
                                sq, sq, AF.Sqrt, bias=EPSG, scale=NEGC),
                                reads=("st3",), writes=("st3",))
                            p.op("dve", lambda e, sq=sq: e.reciprocal(sq, sq),
                                 reads=("st3",), writes=("st3",))
                            def gnn(e, ys=ys, su=su, sq=sq):
                                last = None
                                for h in range(H):
                                    last = e.tensor_scalar(
                                        zt3[:, 64 * h:64 * h + 64],
                                        ys[:, 64 * h:64 * h + 64],
                                        su[:, h:h + 1], sq[:, h:h + 1],
                                        OP.subtract, OP.mult)
                                return last
                            p.op("dve", gnn, reads=("yf", "st3"), writes=("zt3",))
                            p.op("dve", lambda e, bi=bi: e.tensor_mul(
                                gp3[:], g3_tm[:, bi, :], gam_c[:]),
                                reads=("g3", "gam"), writes=("gp",))
                            p.op("dve", lambda e, bi=bi: e.tensor_mul(
                                bp3[:], g3_tm[:, bi, :], bet_c[:]),
                                reads=("g3", "bet"), writes=("bp",))
                            p.op("dve", lambda e, bi=bi: e.tensor_mul(
                                z3_b[:, bi, :], zt3[:], gp3[:]),
                                reads=("zt3", "gp"), writes=("z3",))
                            p.op("dve", lambda e, bi=bi: e.tensor_add(
                                z3_b[:, bi, :], z3_b[:, bi, :], bp3[:]),
                                reads=("z3", "bp"), writes=("z3",))
                            def ztp(e, bi=bi):
                                last = None
                                with nc.allow_low_precision(reason="bf16 transpose"):
                                    for cc in range(8):
                                        last = e.matmul(
                                            psT3[:, 128 * cc:128 * (cc + 1)],
                                            z3_b[:, bi, 128 * cc:128 * (cc + 1)],
                                            i128b_c, start=True, stop=True)
                                return last
                            p.op("pe", ztp, reads=("z3", "i128b"), writes=("psT3",))
                            p.op("dve", lambda e, bi=bi: e.tensor_copy(
                                zf3_b[:, :, 128 * bi:128 * (bi + 1)],
                                psT3.rearrange("p (c n) -> p c n", c=8)),
                                reads=("psT3",), writes=("zf3",))
                        for m in range(TT3A // L):
                            gm = (r0 // L) + m
                            bm = m // 2
                            mp = m % 2
                            pyp = psY3[bm % 2]
                            pyo = 512 * mp
                            pyres = ("psY31", "psY32")[bm % 2]
                            p.op("dve", lambda e, gm=gm: e.tensor_mul(
                                s0m_b[:, :, :], s0_f[:, :, :],
                                dm_fm[:, :, gm:gm + 1]
                                .broadcast_to([128, 8, HS])),
                                reads=("s0", "dm"), writes=("s0m",))
                            def ycmm(e, m=m, pyp=pyp, pyo=pyo, rt3_b=rt3_b):
                                last = None
                                for h in range(H):
                                    ch, hh = h // 2, (h % 2) * 64
                                    last = e.matmul(
                                        pyp[hh:hh + 64, pyo + 64 * ch:pyo + 64 * ch + 64],
                                        rt3_b[hh:hh + 64, ch, L * m:L * (m + 1)],
                                        s0m_b[hh:hh + 64, ch, :],
                                        start=True, stop=True)
                                return last
                            p.op("pe", ycmm, reads=(rtr, "s0m"), writes=(pyres,))
                            mo = 64 * mp
                            def yf1(e, pyp=pyp, pyo=pyo, mo=mo, bm=bm, mp=mp, yl_b=yl_b):
                                yv = yfull[mo:mo + 64, bm, :].rearrange(
                                    "p (c two l) -> p c two l", two=2, l=HS)[:, :, mp, :]
                                lv = yl_b[mo:mo + 64, bm, :].rearrange(
                                    "p (c two l) -> p c two l", two=2, l=HS)[:, :, mp, :]
                                return e.tensor_add(
                                    yv,
                                    pyp[mo:mo + 64, pyo:pyo + 512]
                                    .rearrange("p (c l) -> p c l", l=HS), lv)
                            p.op("dve", yf1, reads=(pyres, ylr), writes=("yf",))
                            def yst3(e, pyp=pyp, pyo=pyo, mp=mp):
                                so = 64 * (1 - mp)
                                return e.tensor_copy(ystg3[so:so + 64, :],
                                                     pyp[so:so + 64, pyo:pyo + 512])
                            p.op("dve", yst3, reads=(pyres,), writes=("ystg3",))
                            p.op("pe", lambda e: e.matmul(psS3[:], selsw_c, ystg3[:],
                                                          start=True, stop=True),
                                 reads=("ystg3", "selsw"), writes=("psS3",))
                            def yf2(e, mo=mo, bm=bm, mp=mp, yl_b=yl_b):
                                yv = yfull[mo:mo + 64, bm, :].rearrange(
                                    "p (c two l) -> p c two l", two=2, l=HS)[:, :, 1 - mp, :]
                                lv = yl_b[mo:mo + 64, bm, :].rearrange(
                                    "p (c two l) -> p c two l", two=2, l=HS)[:, :, 1 - mp, :]
                                return e.tensor_add(
                                    yv,
                                    psS3[mo:mo + 64, 0:512]
                                    .rearrange("p (c l) -> p c l", l=HS), lv)
                            p.op("dve", yf2, reads=("psS3", ylr), writes=("yf",))
                            if m % 2 == 1:
                                gn_block(m // 2)
                        for bi in range(NB3A):
                            pyp = psY3[bi % 2]
                            pyres = ("psY31", "psY32")[bi % 2]
                            def aomm(e, bi=bi, pyp=pyp):
                                last = None
                                for cc in range(8):
                                    for nn in range(2):
                                        last = e.matmul(
                                            pyp[:, 512 * nn:512 * (nn + 1)],
                                            zf3_b[:, cc, 128 * bi:128 * (bi + 1)],
                                            woT_b3[:, cc, 512 * nn:512 * (nn + 1)],
                                            start=(cc == 0), stop=(cc == 7))
                                return last
                            p.op("pe", aomm, reads=("zf3", "wo"), writes=(pyres,))
                            cl = st3a[:, 120:128]
                            p.op("act", lambda e, pyp=pyp, cl=cl: e.activation(
                                ao3[:], pyp[:], AF.Copy, accum_out=cl[:, 0:1]),
                                reads=(pyres,), writes=("ao", "stl2"))
                            p.op("act", lambda e, cl=cl: e.activation(
                                sq3_s[:], ao3[:], AF.Square, accum_out=cl[:, 1:2]),
                                reads=("ao",), writes=("sq3", "stl2"))
                            p.op("dve", lambda e, cl=cl: e.tensor_scalar(
                                cl[:, 2:3], cl[:, 0:1], 1.0 / C, None, OP.mult),
                                reads=("stl2",), writes=("stl2",))
                            p.op("dve", lambda e, cl=cl: e.tensor_scalar(
                                cl[:, 3:4], cl[:, 1:2], 1.0 / C, None, OP.mult),
                                reads=("stl2",), writes=("stl2",))
                            p.op("dve", lambda e, cl=cl: e.scalar_tensor_tensor(
                                cl[:, 3:4], cl[:, 2:3], cl[:, 2:3], cl[:, 3:4],
                                OP.mult, OP.subtract),
                                reads=("stl2",), writes=("stl2",))
                            p.op("act", lambda e, cl=cl: e.activation(
                                cl[:, 4:5], cl[:, 3:4], AF.Sqrt, bias=EPS1,
                                scale=NEGC),
                                reads=("stl2",), writes=("stl2",))
                            p.op("dve", lambda e, cl=cl: e.reciprocal(
                                cl[:, 4:5], cl[:, 4:5]),
                                reads=("stl2",), writes=("stl2",))
                            p.op("dve", lambda e, cl=cl: e.tensor_scalar(
                                cl[:, 5:6], cl[:, 2:3], cl[:, 4:5], -1.0,
                                OP.mult, OP.mult),
                                reads=("stl2",), writes=("stl2",))
                            p.op("act", lambda e, bi=bi, cl=cl: e.activation(
                                l2_b[:, bi, :], ao3[:], AF.Identity, bias=cl[:, 5:6],
                                scale=cl[:, 4:5]),
                                reads=("ao", "stl2"), writes=("l2",))
                        p.dma("pool", W["sp_l2"][r0:r0 + TT3A, :]
                              .rearrange("(b q) f -> q b f", q=128), l2_b[:],
                              reads=("l2",), writes=(), tag="spl2")
                        if t == DBG_TILE3A:
                            for nm, (src, res) in {
                                    "g3": (g3_tm, "g3"), "yfull": (yfull, "yf"),
                                    "z3": (z3_b, "z3"), "l2": (l2_b, "l2"),
                                    "s0": (s0_f, "s0")}.items():
                                da = dbg_out(nm, list(src.shape), src.dtype)
                                p.dma("sp", da, src, reads=(res,), tag="dbg")

                if "3" in PHASES:
                    sems = run_phase(nc, sems, dsem, phase3a, "p3a")

        with ExitStack() as st4:
            sb4 = lambda n, sh, dt=F32: st4.enter_context(nc.sbuf_tensor(n, list(sh), dt)).ap()
            ps4 = lambda n, sh, dt=F32: st4.enter_context(nc.psum_tensor(n, list(sh), dt)).ap()
            wiT_b = sb4("wiT_b", (128, 8, FF), BF16)
            woFT_b = sb4("woFT_b", (128, 32, C), BF16)
            wgateT_b = sb4("wgateT_b", (128, 8, E), BF16)
            bg_c = sb4("bg_c", (128, E))
            aAllT_b = sb4("aAllT_b", (128, 8, 128), BF16)
            bt8P = [sb4(f"bt8P_{pr}", (128, FF), BF16) for pr in range(2)]
            NB3B = TT3B // 128
            x32 = [sb4(f"x3_{i}", (128, NB3B, C)) for i in range(2)]
            l232 = [sb4(f"l23_{i}", (128, NB3B, C), BF16) for i in range(2)]
            x3n_tm = sb4("x3n_tm", (128, NB3B, C), BF16)
            x3nf2 = [sb4(f"x3nf_{i}", (128, 8, TT3B), BF16) for i in range(2)]
            sq4_s = sb4("sq4_s", (128, C), BF16)
            sq4b = sb4("sq4b", (128, C), BF16)
            st4a = sb4("st4a", (128, 32))
            lg4 = sb4("lg4", (128, NB3B, E))
            wt42 = [sb4(f"wt4_{i}", (128, NB3B, E)) for i in range(2)]
            lle8 = [[sb4(f"lle8_{e}_{i}", (128, TT3B), BF16) for e in range(E)]
                    for i in range(2)]
            lle_stg = sb4("lle_stg", (4, TT3B), BF16)
            base_b = sb4("base_b", (128, 2, 512), BF16)
            u4p = [[sb4(f"u4_{fp}_{e}", (128, 512), BF16) for e in range(E)]
                   for fp in range(2)]
            hcT_b = sb4("hcT_b", (128, 2, 512), BF16)
            ffn_f = sb4("ffn_f", (128, C))
            psA4 = ps4("psA4", (128, 512))
            psB4 = [ps4("psB4a", (128, 512)), ps4("psB4b", (128, 512))]
            psS4 = ps4("psS4", (128, 512))
            _psY41 = ps4("psY41", (128, 1024))
            psY4 = [_psY41, _psY41]
            psT4 = ps4("psT4", (128, 512))
            psT4x = ps4("psT4x", (128, 512))

            def phase3b(p):
                p.dma("pool", wiT_b, W["wiT"].rearrange("(c p) n -> p c n", p=128),
                      writes=("wi",))
                p.dma("pool", woFT_b, W["woFT"].rearrange("(c p) n -> p c n", p=128),
                      writes=("wof",))
                p.dma("pool", wgateT_b, W["wgateT"].rearrange("(c p) n -> p c n", p=128),
                      writes=("wgt",))
                p.dma("sp", bg_c, W["bg_bc"], writes=("bg",))
                p.dma("pool", aAllT_b, W["aAllT128"].rearrange("(c p) n -> p c n", p=128),
                      writes=("aA",))
                for pr in range(2):
                    p.dma("pool", bt8P[pr], W[f"bT8P{pr}"], writes=(f"bt{pr}",))
                for i_ in range(2):
                    for e_ in range(E):
                        p.op("dve", lambda e, i_=i_, e_=e_: e.memset(
                            lle8[i_][e_][:], 0.0),
                            writes=((f"lle{e_}", i_),))

                def front_dma(t):
                    tp = t % 2
                    r0 = t * TT3B
                    p.dma("sp", x32[tp], W["xs"][128 + r0:128 + r0 + TT3B, :]
                          .rearrange("(b q) f -> q b f", q=128),
                          writes=(("x3", tp),), tag=f"lx{t}")
                    p.dma("sp", l232[tp], W["sp_l2"][r0:r0 + TT3B, :]
                          .rearrange("(b q) f -> q b f", q=128),
                          writes=(("l23", tp),))

                def front_norm(t):
                    tp = t % 2
                    x2f = x32[tp]
                    xres = ("x3", tp)
                    p.op("dve", lambda e, x2f=x2f, tp=tp: e.tensor_add(
                        x2f[:], x2f[:], l232[tp][:]),
                        reads=(xres, ("l23", tp)), writes=(xres,))
                    rstd, mb, st = ln_stats(p, st4a, 8 * tp, x2f[:, 0, :],
                                            sq4b[:], (xres,), C, EPS1, NEGC,
                                            tag=f"l3{t}")
                    p.op("act", lambda e, x2f=x2f, rstd=rstd, mb=mb: e.activation(
                        x3n_tm[:, 0, :], x2f[:, 0, :], AF.Identity, bias=mb,
                        scale=rstd),
                        reads=(xres, st), writes=("x3n",))

                def front_xtp(t):
                    tp = t % 2
                    xfr = ("x3nf", tp)
                    for q4 in range(2):
                        def xtp(e, q4=q4):
                            last = None
                            with nc.allow_low_precision(reason="bf16 transpose"):
                                for cc in range(4 * q4, 4 * q4 + 4):
                                    last = e.matmul(
                                        psT4x[:, 128 * (cc % 4):128 * (cc % 4 + 1)],
                                        x3n_tm[:, 0, 128 * cc:128 * (cc + 1)],
                                        i128b_c, start=True, stop=True)
                            return last
                        p.op("pe", xtp, reads=("x3n", "i128b"), writes=("psT4x",))
                        p.op("dve", lambda e, q4=q4, tp=tp: e.tensor_copy(
                            x3nf2[tp][:, 4 * q4:4 * q4 + 4, :],
                            psT4x.rearrange("p (c n) -> p c n", c=4)),
                            reads=("psT4x",), writes=(xfr,))

                def gate_a(t):
                    tp = t % 2
                    xf = x3nf2[tp]
                    xfr = ("x3nf", tp)
                    wt4 = wt42[tp]
                    wtres = ("wt", tp)

                    def lgm(e, xf=xf):
                        last = None
                        for cc in range(8):
                            last = e.matmul(psS4[:, 0:4], xf[:, cc, :],
                                            wgateT_b[:, cc, :],
                                            start=(cc == 0), stop=(cc == 7))
                        return last
                    p.op("pe", lgm, reads=(xfr, "wgt"), writes=("psS4",))
                    p.op("dve", lambda e: e.tensor_add(
                        lg4[:], psS4[:, 0:4 * NB3B].rearrange("p (b e) -> p b e",
                                                              b=NB3B),
                        bg_c[:, :].unsqueeze(1).broadcast_to([128, NB3B, E])),
                        reads=("psS4", "bg"), writes=("lg",))
                    sc = lg4[:, 0, :]
                    mx = st4a[:, 16:17]
                    sm = st4a[:, 17:18]
                    p.op("dve", lambda e, sc=sc, mx=mx: e.reduce_max(
                        mx, sc, axis=AX.X, negate=True),
                        reads=("lg",), writes=("st4",))
                    p.op("act", lambda e, sc=sc, mx=mx: e.activation(
                        sc, sc, AF.Exp, bias=mx),
                        reads=("lg", "st4"), writes=("lg",))
                    p.op("dve", lambda e, sc=sc, sm=sm: e.reduce_sum(
                        sm, sc, axis=AX.X),
                        reads=("lg",), writes=("st4",))
                    p.op("dve", lambda e, sm=sm: e.reciprocal(sm, sm),
                         reads=("st4",), writes=("st4",))
                    p.op("dve", lambda e, sc=sc, sm=sm: e.tensor_scalar(
                        sc, sc, sm, None, OP.mult),
                        reads=("lg", "st4"), writes=("lg",))
                    a_, b_, c_, d_ = (sc[:, 0:1], sc[:, 1:2], sc[:, 2:3], sc[:, 3:4])
                    t0 = st4a[:, 18:19]; t1 = st4a[:, 19:20]
                    t2 = st4a[:, 20:21]; t3 = st4a[:, 21:22]
                    sec = st4a[:, 22:23]
                    for (dst, xx, yy, op) in ((t0, a_, b_, OP.max),
                                              (t1, a_, b_, OP.min),
                                              (t2, c_, d_, OP.max),
                                              (t3, c_, d_, OP.min)):
                        p.op("dve", lambda e, dst=dst, xx=xx, yy=yy, op=op:
                             e.tensor_tensor(dst, xx, yy, op),
                             reads=("lg", "st4"), writes=("st4",))
                    p.op("dve", lambda e: e.tensor_tensor(t0, t0, t2, OP.min),
                         reads=("st4",), writes=("st4",))
                    p.op("dve", lambda e: e.tensor_max(t1, t1, t3),
                         reads=("st4",), writes=("st4",))
                    p.op("dve", lambda e: e.tensor_max(sec, t0, t1),
                         reads=("st4",), writes=("st4",))
                    p.op("dve", lambda e, sc=sc, wt4=wt4: e.tensor_scalar(
                        wt4[:, 0, :], sc, sec, None, OP.subtract),
                        reads=("lg", "st4"), writes=(wtres,))
                    p.op("act", lambda e, wt4=wt4: e.activation(
                        wt4[:, 0, :], wt4[:, 0, :], AF.Sign),
                        reads=(wtres,), writes=(wtres,))
                    p.op("dve", lambda e, wt4=wt4: e.tensor_scalar(
                        wt4[:, 0, :], wt4[:, 0, :], 1.0, 1.0, OP.add, OP.min),
                        reads=(wtres,), writes=(wtres,))
                    p.op("dve", lambda e, sc=sc, wt4=wt4: e.tensor_mul(
                        wt4[:, 0, :], wt4[:, 0, :], sc),
                        reads=(wtres, "lg"), writes=(wtres,))

                def gate_b(t):
                    tp = t % 2
                    xf = x3nf2[tp]
                    xfr = ("x3nf", tp)
                    lle = lle8[tp]

                    def llm(e, xf=xf):
                        last = None
                        for e_ in range(E):
                            for cc in range(8):
                                last = e.matmul(psS4[0:4, TT3B * e_:TT3B * (e_ + 1)],
                                                aAllT_b[:, cc, 32 * e_:32 * e_ + 4],
                                                xf[:, cc, :],
                                                start=(cc == 0), stop=(cc == 7))
                        return last
                    p.op("pe", llm, reads=(xfr, "aA"), writes=("psS4",))
                    for e_ in range(E):
                        if e_ % 2 == 0:
                            p.op("act", lambda e, e_=e_, lle=lle: e.activation(
                                lle[e_][0:4, :],
                                psS4[0:4, TT3B * e_:TT3B * (e_ + 1)], AF.Copy),
                                reads=("psS4",), writes=((f"lle{e_}", tp),))
                        else:
                            p.op("act", lambda e, e_=e_: e.activation(
                                lle_stg[:],
                                psS4[0:4, TT3B * e_:TT3B * (e_ + 1)], AF.Copy),
                                reads=("psS4",), writes=("lstg",))
                            p.dma("act", lle[e_][4:8, :], lle_stg[:],
                                  reads=("lstg",), writes=((f"lle{e_}", tp),))

                def back(t):
                    tp = t % 2
                    r0 = t * TT3B
                    xres = ("x3", tp)
                    xfr = ("x3nf", tp)
                    xf = x3nf2[tp]
                    x2f = x32[tp]
                    wt4 = wt42[tp]
                    wtres = ("wt", tp)
                    lle = lle8[tp]
                    if t + 1 < NT3B:
                        front_dma(t + 1)
                    pyp = psY4[0]
                    pyres = "psY41"

                    def emit_bmm(fc):
                        def bmm(e, fc=fc, xf=xf):
                            last = None
                            for cc in range(8):
                                last = e.matmul(
                                    psA4[:], xf[:, cc, :],
                                    wiT_b[:, cc, 512 * fc:512 * (fc + 1)],
                                    start=(cc == 0), stop=(cc == 7))
                            return last
                        p.op("pe", bmm, reads=(xfr, "wi"), writes=("psA4",))
                        p.op("act", lambda e, fp=fc % 2: e.activation(
                            base_b[:, fp, :], psA4[:], AF.Copy),
                            reads=("psA4",), writes=(("baseb", fc % 2),))

                    def emit_emm_ts(fc):
                        fp = fc % 2
                        for ei in range(E):
                            ep = ei % 2
                            pbx = psB4[ep]
                            pbres = ("psB4a", "psB4b")[ep]
                            def emm(e, fc=fc, ei=ei, fp=fp, pbx=pbx, lle=lle):
                                e.matmul(pbx[:], lle[ei][:, :],
                                         bt8P[ei // 2][:, 512 * fc:512 * (fc + 1)],
                                         start=True, stop=False)
                                return e.matmul(pbx[:], i128b_c, base_b[:, fp, :],
                                                start=False, stop=True)
                            p.op("pe", emm,
                                 reads=(("baseb", fp), (f"lle{ei}", tp),
                                        f"bt{ei // 2}", "i128b"),
                                 writes=(pbres,))
                            ub = u4p[fp][ei]
                            ubres = (f"u4_{ei}", fp)
                            if ei in (0, 2):
                                p.op("dve", lambda e, ei=ei, pbx=pbx, wt4=wt4,
                                     ub=ub:
                                     e.tensor_scalar(ub[:], pbx[:],
                                                     wt4[:, 0, ei:ei + 1], 0.0,
                                                     OP.mult, OP.max),
                                     reads=(pbres, wtres), writes=(ubres,))
                            else:
                                p.op("act", lambda e, ei=ei, pbx=pbx, wt4=wt4,
                                     ub=ub:
                                     e.activation(ub[:], pbx[:], AF.Relu,
                                                  scale=wt4[:, 0, ei:ei + 1]),
                                     reads=(pbres, wtres), writes=(ubres,))

                    def emit_hct(fc):
                        fp = fc % 2
                        def hct(e, fp=fp):
                            last = None
                            with nc.allow_low_precision(reason="bf16 transpose"):
                                for e_ in range(E):
                                    for j in range(4):
                                        last = e.matmul(
                                            psT4[:, 128 * j:128 * (j + 1)],
                                            u4p[fp][e_][:, 128 * j:128 * (j + 1)],
                                            i128b_c,
                                            start=(e_ == 0 and j == 0),
                                            stop=(e_ == 3 and j == 3))
                            return last
                        p.op("pe", hct,
                             reads=tuple((f"u4_{e_}", fp) for e_ in range(E))
                             + ("i128b",),
                             writes=("psT4",))
                        p.op("dve", lambda e, fp=fp: e.tensor_copy(
                            hcT_b[:, fp, :], psT4[:]),
                            reads=("psT4",), writes=(("hcT", fp),))

                    def emit_omm(fc):
                        fp = fc % 2
                        def omm(e, fc=fc, pyp=pyp, fp=fp):
                            last = None
                            for j in range(4):
                                for nn in range(2):
                                    last = e.matmul(
                                        pyp[:, 512 * nn:512 * (nn + 1)],
                                        hcT_b[:, fp, 128 * j:128 * (j + 1)],
                                        woFT_b[:, 4 * fc + j,
                                               512 * nn:512 * (nn + 1)],
                                        start=(fc == 0 and j == 0),
                                        stop=(fc == 7 and j == 3))
                            return last
                        p.op("pe", omm, reads=(("hcT", fp), "wof"),
                             writes=(pyres,))

                    emit_bmm(0)
                    emit_emm_ts(0)
                    for fc in range(8):
                        if fc < 7:
                            emit_bmm(fc + 1)
                        emit_hct(fc)
                        if fc < 7:
                            emit_emm_ts(fc + 1)
                        emit_omm(fc)
                        if t + 1 < NT3B:
                            if fc == 1:
                                front_norm(t + 1)
                            elif fc == 3:
                                front_xtp(t + 1)
                            elif fc == 5:
                                gate_a(t + 1)
                            elif fc == 6:
                                gate_b(t + 1)
                    cl = st4a[:, 24:32]
                    p.op("act", lambda e, pyp=pyp, cl=cl: e.activation(
                        ffn_f[:], pyp[:], AF.Copy, accum_out=cl[:, 0:1]),
                        reads=(pyres,), writes=("ffn", "stl4"))
                    p.op("act", lambda e, cl=cl: e.activation(
                        sq4_s[:], ffn_f[:], AF.Square, accum_out=cl[:, 1:2]),
                        reads=("ffn",), writes=("sq4", "stl4"))
                    p.op("dve", lambda e, cl=cl: e.tensor_scalar(
                        cl[:, 2:3], cl[:, 0:1], 1.0 / C, None, OP.mult),
                        reads=("stl4",), writes=("stl4",))
                    p.op("dve", lambda e, cl=cl: e.tensor_scalar(
                        cl[:, 3:4], cl[:, 1:2], 1.0 / C, None, OP.mult),
                        reads=("stl4",), writes=("stl4",))
                    p.op("dve", lambda e, cl=cl: e.scalar_tensor_tensor(
                        cl[:, 3:4], cl[:, 2:3], cl[:, 2:3], cl[:, 3:4],
                        OP.mult, OP.subtract),
                        reads=("stl4",), writes=("stl4",))
                    p.op("act", lambda e, cl=cl: e.activation(
                        cl[:, 4:5], cl[:, 3:4], AF.Sqrt, bias=EPS1, scale=NEGC),
                        reads=("stl4",), writes=("stl4",))
                    p.op("dve", lambda e, cl=cl: e.reciprocal(cl[:, 4:5],
                                                              cl[:, 4:5]),
                         reads=("stl4",), writes=("stl4",))
                    p.op("dve", lambda e, cl=cl: e.tensor_scalar(
                        cl[:, 5:6], cl[:, 2:3], cl[:, 4:5], -1.0,
                        OP.mult, OP.mult),
                        reads=("stl4",), writes=("stl4",))
                    p.op("dve", lambda e, cl=cl, x2f=x2f: e.scalar_tensor_tensor(
                        x2f[:, 0, :], ffn_f[:], cl[:, 4:5], x2f[:, 0, :],
                        OP.mult, OP.add),
                        reads=("ffn", "stl4", xres), writes=(xres,))
                    p.op("dve", lambda e, cl=cl, x2f=x2f: e.tensor_scalar(
                        x2f[:, 0, :], x2f[:, 0, :], cl[:, 5:6], None,
                        OP.add),
                        reads=(xres, "stl4"), writes=(xres,))
                    p.dma("sp", out_ap[r0:r0 + TT3B, :]
                          .rearrange("(b q) f -> q b f", q=128),
                          x2f[:], reads=(xres,), writes=("outdma",), tag="out")

                front_dma(0)
                front_norm(0)
                front_xtp(0)
                gate_a(0)
                gate_b(0)
                for t in range(NT3B):
                    back(t)

            if "4" in PHASES:
                sems = run_phase(nc, sems, dsem, phase3b, "p3b")

    return nc, dbg_names


def _prep_inputs(inp):
    x = np.asarray(inp["L0_attn_x"])
    A_lora = np.asarray(inp["L0_attn_A_lora"])
    B_lora = np.asarray(inp["L0_attn_B_lora"])
    lam = np.asarray(inp["L0_attn_lambda_"]).reshape(5, C)
    shared = {
        "wkT": np.ascontiguousarray(np.asarray(inp["L0_attn_Wk"]).T),
        "wrT": np.ascontiguousarray(np.asarray(inp["L0_attn_Wr"]).T),
        "wvT": np.ascontiguousarray(np.asarray(inp["L0_attn_Wv"]).T),
        "wgT": np.ascontiguousarray(np.asarray(inp["L0_attn_Wg"]).T),
        "woT": np.ascontiguousarray(np.asarray(inp["L0_attn_Wo"]).T),
        "alora": np.ascontiguousarray(A_lora),
        "blora33": np.ascontiguousarray(
            np.concatenate([B_lora, lam[:, None, :]], axis=1)),
        "tdA": np.ascontiguousarray(np.asarray(inp["L0_attn_td_A"])),
        "tdB65": np.ascontiguousarray(np.concatenate(
            [np.asarray(inp["L0_attn_td_B"]),
             np.asarray(inp["L0_attn_td_miu"]).reshape(1, C)], axis=0)),
        "miux_col": np.ascontiguousarray(
            np.asarray(inp["L0_attn_miu_x"]).reshape(8, 128).T),
        "u_col": np.ascontiguousarray(
            np.asarray(inp["L0_attn_u"]).reshape(8, 128).T),
        "gam_bc": np.ascontiguousarray(np.broadcast_to(
            np.asarray(inp["L0_attn_gn_gamma"]).reshape(1, C), (128, C))),
        "bet_bc": np.ascontiguousarray(np.broadcast_to(
            np.asarray(inp["L0_attn_gn_beta"]).reshape(1, C), (128, C))),
        "maskU2": np.ascontiguousarray(np.tile(
            np.triu(np.ones((L, L), np.float32), 1), (2, 1))),
        "i128f": np.eye(128, dtype=np.float32),
        "i128b": np.eye(128, dtype=BFD),
        "selswap": np.ascontiguousarray(np.roll(np.eye(128, dtype=BFD), 64, axis=1)),
        "wiT": np.ascontiguousarray(np.asarray(inp["L0_ffn_Wi"]).T),
        "woFT": np.ascontiguousarray(np.asarray(inp["L0_ffn_Wo"]).T),
        "wgateT": np.ascontiguousarray(np.asarray(inp["L0_ffn_Wg"]).T),
        "bg_bc": np.ascontiguousarray(np.broadcast_to(
            np.asarray(inp["L0_ffn_bg"]).reshape(1, E), (128, E))),
    }
    hind = np.zeros((8, 128, 16), BFD)
    for cc in range(8):
        for pp in range(128):
            hind[cc, pp, (128 * cc + pp) // HS] = 1
    shared["hind8"] = hind
    cvec = np.zeros((128, 4), np.float32)
    cvec[:, 0] = -1.0
    cvec[:, 1] = 1e-5
    cvec[:, 2] = 1e-5 * H
    shared["cvec"] = cvec
    aT = np.zeros((C, 128), np.float32)
    Am = np.asarray(inp["L0_ffn_A"])
    for e in range(E):
        aT[:, 32 * e:32 * e + 4] = Am[e].T
    shared["aAllT128"] = aT
    Bm = np.asarray(inp["L0_ffn_B"])
    for e in range(E):
        pass
    for pr in range(2):
        btp = np.zeros((128, FF), np.float32)
        btp[0:4] = Bm[2 * pr].T
        btp[4:8] = Bm[2 * pr + 1].T
        shared[f"bT8P{pr}"] = btp

    per_core = []
    for c in range(NCORE):
        bb, ppos = c // 4, c % 4
        s0 = ppos * SLAB
        xs = np.zeros((128 + SLAB, C), np.float32)
        xs[128:] = x[bb, s0:s0 + SLAB]
        if ppos > 0:
            xs[:128] = x[bb, s0 - 128:s0]
        pm = np.zeros((128, 8), np.float32)
        for q in range(8):
            if q // 4 == bb and q % 4 < ppos:
                pm[:, q] = 1.0
        m = dict(shared)
        m["xs"] = xs
        m["prevmask"] = pm
        per_core.append(m)
    return per_core


def kernel(**inp):
    per_core = _prep_inputs(inp)
    nc, dbg_names = build_nc()
    res = run_bass_kernel_spmd(nc, per_core, core_ids=list(range(NCORE)),
                               trace=TRACE_RUN)
    out = np.zeros((B, T, C), np.float32)
    for c in range(NCORE):
        bb, ppos = c // 4, c % 4
        if "out" in res.results[c]:
            out[bb, ppos * SLAB:(ppos + 1) * SLAB] = res.results[c]["out"]
    kernel._last = res
    return out
